# version 02
# baseline (speedup 1.0000x reference)
import math
from contextlib import ExitStack

import numpy as np
import concourse.bass as bass
import concourse.mybir as mybir
from concourse.bass_utils import run_bass_kernel_spmd

F32 = mybir.dt.float32
BF16 = mybir.dt.bfloat16
I32 = mybir.dt.int32
AF = mybir.ActivationFunctionType
ALU = mybir.AluOpType
AX = mybir.AxisListType

NCORES = 8
B = 2
S = 2048
D = 2048
H = 16
DH = 128
DT = D // 128
R = 16
NL = 4
E = 16
I_MOE = 1408
IS = 2816
TCORE = B * S
TBLK = 1024
NBLK = TCORE // TBLK
SCALE = 1.0 / math.sqrt(DH)
EXP_BIAS = -8.0
EPS = 1e-5
THETA = 10000.0


class Sem:
    def __init__(self, nc, name):
        self.h = nc.alloc_semaphore(name)
        self.c = 0

    def tok(self):
        return (self, self.c)


class Sched:
    def __init__(self):
        self.items = []

    def op(self, eng, fn, waits=(), inc=None, inc_n=1):
        ws = [w for w in waits if w is not None]
        for (sem, val) in ws:
            assert val <= sem.c, f"wait on future sem value {val} > {sem.c}"
        tok = None
        if inc is not None:
            inc.c += inc_n
            tok = inc.tok()
        self.items.append((eng, fn, tuple(ws), (inc, inc_n) if inc else None))
        return tok

    def dma(self, eng, out_ap, in_ap, waits=(), sem=None, transpose=False):
        if transpose:
            def fn(e, o=out_ap, i=in_ap):
                return e.dma_start(o, i, transpose=True)
        else:
            def fn(e, o=out_ap, i=in_ap):
                return e.dma_start(o, i)
        return self.op(eng, fn, waits, inc=sem, inc_n=16)

    def emit(self, block):
        items = self.items

        def mk(engname):
            def run(eng):
                waited = {}
                for (en, fn, waits, inc) in items:
                    if en != engname:
                        continue
                    agg = {}
                    for (sem, val) in waits:
                        if val > agg.get(id(sem), (None, -1))[1]:
                            agg[id(sem)] = (sem, val)
                    for (sem, val) in agg.values():
                        if val <= 0 or waited.get(id(sem), -1) >= val:
                            continue
                        eng.wait_ge(sem.h, val)
                        waited[id(sem)] = val
                    ins = fn(eng)
                    if inc is not None:
                        ins.then_inc(inc[0].h, inc[1])
            return run

        block.tensor(mk("t"))
        block.vector(mk("v"))
        block.scalar(mk("s"))
        block.gpsimd(mk("g"))
        block.sync(mk("y"))


def build(dbg=False):
    nc = bass.Bass("TRN2", target_bir_lowering=False)

    def din(name, shape, dt=F32):
        return nc.dram_tensor(name, list(shape), dt, kind="ExternalInput").ap()

    x_in = din("L0_attn_x", (B, S, D))
    starts_in = din("L0_attn_lora_start_indices", (1, NL), I32)
    W_in, A_in, B_in = {}, {}, {}
    for p, nm in enumerate("qkvo"):
        W_in[p] = din(f"L0_attn_W{nm}", (D, D))
        A_in[p] = din(f"L0_attn_A{nm}", (NL, R, D))
        B_in[p] = din(f"L0_attn_B{nm}", (NL, D, R))
    gate_in = din("L0_ffn_gate_w", (E, D))
    w1_in = din("L0_ffn_w1", (E, 2 * I_MOE, D))
    w2_in = din("L0_ffn_w2", (E, D, I_MOE))
    s1_in = din("L0_ffn_shared_w1", (2 * IS, D))
    s2_in = din("L0_ffn_shared_w2", (D, IS))

    iden_in = din("c_iden", (128, 128))
    idenb_in = din("c_idenb", (128, 128), BF16)
    perm_in = din("c_perm", (128, 128))
    ropeC_in = din("c_ropec", (128, S), BF16)
    ropeS_in = din("c_ropes", (128, S), BF16)
    mask_in = din("c_mask", (128, 4, 512), BF16)
    eall_in = din("c_eall", (E, E * 128))
    erow_in = din("c_erow", (16, B * 128))
    ones16_in = din("c_ones16", (1, 16))
    ones128_in = din("c_ones128", (1, 128))
    onescol_in = din("c_onescol", (128, 1), BF16)
    iota16_in = din("c_iota16", (16, 1))
    iota4r_in = din("c_iota4r", (16, NL))

    def dint(name, shape, dt=BF16):
        return nc.dram_tensor(name, list(shape), dt, kind="Internal").ap()

    wT_sc = {p: dint(f"w{nm}T_sc", (H, 128, DT, 128))
             for p, nm in enumerate("qkvo")}
    w1T_sc = dint("w1T_sc", (E, 22, 128, DT, 128))
    w2T_sc = dint("w2T_sc", (E, I_MOE // 128, 128, DT, 128))
    s1T_sc = dint("s1T_sc", (44, 128, DT, 128))
    s2T_sc = dint("s2T_sc", (IS // 128, 128, DT, 128))
    gw_sc = dint("gw_sc", (DT, E, 128))
    xn_sc = dint("xn_sc", (TCORE, D))
    beff_sc = dint("beff_sc", (4 * B, R, D))
    comb_sc = dint("comb_sc", (E, TCORE), F32)
    attnT_sc = dint("attnT_sc", (B, H, 128, S))
    attn_sc = dint("attn_sc", (DT, 128, TCORE), F32)

    out_t = nc.dram_tensor("out", [B, S, D], F32, kind="ExternalOutput").ap()
    out_flat = out_t.rearrange("b s d -> (b s) d")
    x_flat = x_in.rearrange("b s d -> (b s) d")

    dbg_outs = {}
    if dbg:
        dbg_outs["o_xn"] = nc.dram_tensor("o_xn", [TCORE, D], BF16, kind="ExternalOutput").ap()
        dbg_outs["o_attn"] = nc.dram_tensor("o_attn", [DT, 128, TCORE], F32, kind="ExternalOutput").ap()
        dbg_outs["o_comb"] = nc.dram_tensor("o_comb", [E, TCORE], F32, kind="ExternalOutput").ap()
        dbg_outs["o_ah"] = nc.dram_tensor("o_ah", [B, H, 128, S], BF16, kind="ExternalOutput").ap()
        dbg_outs["o_qh"] = nc.dram_tensor("o_qh", [128, S], BF16, kind="ExternalOutput").ap()
        dbg_outs["o_kh"] = nc.dram_tensor("o_kh", [128, S], BF16, kind="ExternalOutput").ap()
        dbg_outs["o_vh"] = nc.dram_tensor("o_vh", [128, DT, 128], BF16, kind="ExternalOutput").ap()

    sch = Sched()
    tP = Sem(nc, "tP")
    vP = Sem(nc, "vP")
    sP = Sem(nc, "sP")
    _sembank = {}

    def sem(name):
        if name not in _sembank:
            _sembank[name] = Sem(nc, name)
        return _sembank[name]

    with ExitStack() as top:
        ent = top.enter_context

        iden = ent(nc.sbuf_tensor("iden", [128, 128], F32))
        idenb = ent(nc.sbuf_tensor("idenb", [128, 128], BF16))
        perm = ent(nc.sbuf_tensor("perm", [128, 128], F32))
        ones16 = ent(nc.sbuf_tensor("ones16", [1, 16], F32))
        ones128 = ent(nc.sbuf_tensor("ones128", [1, 128], F32))
        onescol = ent(nc.sbuf_tensor("onescol", [128, 1], BF16))
        iota16 = ent(nc.sbuf_tensor("iota16", [16, 1], F32))
        iota4r = ent(nc.sbuf_tensor("iota4r", [16, NL], F32))
        erow = ent(nc.sbuf_tensor("erow", [16, B * 128], F32))
        gate_wT = ent(nc.sbuf_tensor("gate_wT", [128, DT, E], BF16, is_transpose_dest=True))
        aeffT = ent(nc.sbuf_tensor("aeffT", [128, 4 * B * DT, R], BF16))
        neg8 = ent(nc.sbuf_tensor("neg8", [128, 1], F32))
        xt = ent(nc.sbuf_tensor("xt", [128, D], F32))
        xnt = ent(nc.sbuf_tensor("xnt", [128, D], BF16))
        lnst = ent(nc.sbuf_tensor("lnst", [128, 6], F32))

        for sb, src in ((iden, iden_in), (idenb, idenb_in), (perm, perm_in),
                        (ones16, ones16_in), (ones128, ones128_in),
                        (onescol, onescol_in), (iota16, iota16_in),
                        (iota4r, iota4r_in), (erow, erow_in)):
            sch.dma("g", sb[:, :], src[:, :], sem=sem("c0"))
        for dt in range(DT):
            sch.dma("g", gw_sc[dt], gate_in[:, dt * 128:(dt + 1) * 128], sem=sem("c0"))
        p0a = sem("c0").tok()
        neg8_ok = sch.op("v", lambda e: e.memset(neg8[:, :], float(EXP_BIAS)), inc=vP)

        for dt in range(DT):
            sch.dma("y", gate_wT[:, dt, :], gw_sc[dt], waits=[p0a],
                    sem=sem("gwT"), transpose=True)
        gateT_ok = sem("gwT").tok()

        with ExitStack() as p0:
            e0 = p0.enter_context
            a_sb = e0(nc.sbuf_tensor("a_sb", [R, NL, DT, 128], F32))
            b_sb = e0(nc.sbuf_tensor("b_sb", [128, DT, NL, R], F32))
            st_i = e0(nc.sbuf_tensor("st_i", [1, NL], I32))
            st_f = e0(nc.sbuf_tensor("st_f", [1, NL], F32))
            stb = e0(nc.sbuf_tensor("stb", [16, NL], F32))
            ids = e0(nc.sbuf_tensor("ids", [16, 2], F32))
            oh = e0(nc.sbuf_tensor("oh", [16, NL], F32))
            ohrep = e0(nc.sbuf_tensor("ohrep", [128, B * NL], F32))
            aeff = e0(nc.sbuf_tensor("aeff", [R, DT, 128], F32))
            beffn = e0(nc.sbuf_tensor("beffn", [128, DT, R], F32))
            beffst = e0(nc.sbuf_tensor("beffst", [R, D], BF16))
            ps0 = e0(nc.psum_tensor("ps0", [128, 512], F32))

            tst = sch.dma("g", st_i[:, :], starts_in[:, :], sem=sem("st"))
            c1 = sch.op("v", lambda e: e.tensor_copy(st_f[:, :], st_i[:, :]),
                        waits=[tst], inc=vP)
            c2 = sch.op("t", lambda e: e.matmul(ps0[0:16, 0:NL], ones16[:, :],
                                                st_f[:, :], start=True, stop=True),
                        waits=[c1, p0a], inc=tP)
            c3 = sch.op("v", lambda e: e.tensor_copy(stb[:, :], ps0[0:16, 0:NL]),
                        waits=[c2], inc=vP)
            c4 = sch.op("v", lambda e: e.tensor_scalar(stb[:, :], stb[:, :],
                                                       iota16[:, 0:1], None,
                                                       op0=ALU.is_le),
                        waits=[c3], inc=vP)
            c5 = sch.op("v", lambda e: e.reduce_sum(ids[:, 0:1], stb[:, :], axis=AX.X),
                        waits=[c4], inc=vP)
            c6 = sch.op("v", lambda e: e.tensor_scalar(ids[:, 1:2], ids[:, 0:1],
                                                       -1.0, 0.0, op0=ALU.add,
                                                       op1=ALU.max),
                        waits=[c5], inc=vP)
            c7 = sch.op("v", lambda e: e.tensor_scalar(ids[:, 1:2], ids[:, 1:2],
                                                       3.0, None, op0=ALU.min),
                        waits=[c6], inc=vP)
            c8 = sch.op("v", lambda e: e.tensor_scalar(oh[:, :], iota4r[:, :],
                                                       ids[:, 1:2], None,
                                                       op0=ALU.is_equal),
                        waits=[c7], inc=vP)
            c9s = []
            for b in range(B):
                tt_ = sch.op("t", lambda e, b=b: e.matmul(
                    ps0[:, 256 + b * NL:256 + (b + 1) * NL],
                    erow[:, b * 128:(b + 1) * 128], oh[:, :],
                    start=True, stop=True), waits=[c8, c3], inc=tP)
                c9s.append(tt_)
            c10 = sch.op("v", lambda e: e.tensor_copy(ohrep[:, :],
                                                      ps0[:, 256:256 + B * NL]),
                         waits=c9s, inc=vP)

            ps_free = c10
            next_a = None
            next_b = None
            bst_free = None
            for p in range(4):
                ta = tb = None
                for j in range(NL):
                    ta = sch.dma("g", a_sb[:, j, :, :],
                                 A_in[p][j].rearrange("r (dt c) -> r dt c", c=128),
                                 waits=[next_a] if j == 0 else [], sem=sem("a"))
                    tb = sch.dma("g", b_sb[:, :, j, :],
                                 B_in[p][j].rearrange("(ot pp) r -> pp ot r", pp=128),
                                 waits=[next_b] if j == 0 else [], sem=sem("b"))
                for b in range(B):
                    pb = p * B + b
                    prev = None
                    for j in range(NL):
                        if j == 0:
                            prev = sch.op("v", lambda e, j=j, b=b:
                                          e.tensor_scalar(aeff[:, :, :], a_sb[:, j, :, :],
                                                          ohrep[0:16, b * NL + j:b * NL + j + 1],
                                                          None, op0=ALU.mult),
                                          waits=[c10, ta, next_a], inc=vP)
                        else:
                            prev = sch.op("v", lambda e, j=j, b=b:
                                          e.scalar_tensor_tensor(aeff[:, :, :], a_sb[:, j, :, :],
                                                                 ohrep[0:16, b * NL + j:b * NL + j + 1],
                                                                 aeff[:, :, :],
                                                                 op0=ALU.mult, op1=ALU.add),
                                          waits=[prev], inc=vP)
                    for c in range(DT):
                        tt_ = sch.op("t", lambda e, c=c: e.transpose(
                            ps0[:, 0:R], aeff[:, c, :], iden[0:16, 0:16]),
                            waits=[prev, ps_free], inc=tP)
                        ps_free = sch.op("v", lambda e, c=c, pb=pb: e.tensor_copy(
                            aeffT[:, pb * DT + c, :], ps0[:, 0:R]),
                            waits=[tt_], inc=vP)
                    next_a = ps_free
                    prev2 = None
                    for j in range(NL):
                        if j == 0:
                            prev2 = sch.op("v", lambda e, j=j, b=b:
                                           e.tensor_scalar(beffn[:, :, :], b_sb[:, :, j, :],
                                                           ohrep[:, b * NL + j:b * NL + j + 1],
                                                           None, op0=ALU.mult),
                                           waits=[c10, tb, next_b], inc=vP)
                        else:
                            prev2 = sch.op("v", lambda e, j=j, b=b:
                                           e.scalar_tensor_tensor(beffn[:, :, :], b_sb[:, :, j, :],
                                                                  ohrep[:, b * NL + j:b * NL + j + 1],
                                                                  beffn[:, :, :],
                                                                  op0=ALU.mult, op1=ALU.add),
                                           waits=[prev2], inc=vP)
                    for ot in range(DT):
                        tt_ = sch.op("t", lambda e, ot=ot: e.transpose(
                            ps0[0:R, 0:128], beffn[:, ot, :], iden[:, :]),
                            waits=[prev2, ps_free], inc=tP)
                        ps_free = sch.op("v", lambda e, ot=ot: e.tensor_copy(
                            beffst[:, ot * 128:(ot + 1) * 128], ps0[0:R, 0:128]),
                            waits=[tt_] + ([bst_free] if ot == 0 else []), inc=vP)
                    bst_free = sch.dma("g", beff_sc[pb], beffst[:, :],
                                       waits=[ps_free], sem=sem("bst"))
                    next_b = ps_free
            lora_ok_v = vP.tok()
            lora_ok_t = tP.tok()
            beff_ok = sem("bst").tok()

        NT = TCORE // 128
        x_free = None
        xn_free = None
        prev_a4 = None
        prev_a9 = None
        for i in range(NT):
            rows = slice(i * 128, (i + 1) * 128)
            ld = sch.dma("s", xt[:, :], x_flat[rows, :],
                         waits=[x_free] if i else
                         [lora_ok_v, lora_ok_t, beff_ok], sem=sem("x1"))
            a1 = sch.op("v", lambda e: e.reduce_sum(
                lnst[:, 0:1], xt[:, :], axis=AX.X), waits=[ld], inc=vP)
            a2 = sch.op("s", lambda e: e.activation(
                xnt[:, :], xt[:, :], AF.Square, accum_out=lnst[:, 1:2]),
                waits=[ld, prev_a4, xn_free], inc=sP)
            a3 = sch.op("v", lambda e: e.tensor_scalar(
                lnst[:, 2:3], lnst[:, 0:1], 1.0 / D, None, op0=ALU.mult),
                waits=[a1], inc=vP)
            a4 = sch.op("v", lambda e: e.tensor_scalar(
                lnst[:, 3:4], lnst[:, 1:2], 1.0 / D, float(EPS),
                op0=ALU.mult, op1=ALU.add),
                waits=[a2], inc=vP)
            a5 = sch.op("v", lambda e: e.tensor_mul(
                lnst[:, 5:6], lnst[:, 2:3], lnst[:, 2:3]), waits=[a3], inc=vP)
            a6 = sch.op("v", lambda e: e.tensor_sub(
                lnst[:, 3:4], lnst[:, 3:4], lnst[:, 5:6]), waits=[a4, a5], inc=vP)
            a7 = sch.op("s", lambda e: e.activation(
                lnst[:, 4:5], lnst[:, 3:4], AF.Ln),
                waits=[a6, prev_a9], inc=sP)
            a8 = sch.op("s", lambda e: e.activation(
                lnst[:, 4:5], lnst[:, 4:5], AF.Exp, bias=0.0, scale=-0.5),
                waits=[a7], inc=sP)
            a9 = sch.op("v", lambda e: e.tensor_scalar(
                xnt[:, :], xt[:, :], lnst[:, 2:3], lnst[:, 4:5],
                op0=ALU.subtract, op1=ALU.mult),
                waits=[a8], inc=vP)
            x_free = a9
            prev_a4 = a4
            prev_a9 = a9
            st = sch.dma("s", xn_sc[rows, :], xnt[:, :], waits=[a9], sem=sem("xnst"))
            xn_free = st
        p1_done = sem("xnst").tok()
        if dbg:
            sch.dma("g", dbg_outs["o_xn"][:, :], xn_sc[:, :],
                    waits=[p1_done], sem=sem("dbg"))

        with ExitStack() as p2:
            e2 = p2.enter_context
            NFIN = 2
            fin = ent2 = [e2(nc.sbuf_tensor(f"fin{i}", [128, DT, 128], BF16))
                          for i in range(NFIN)]
            fout = [e2(nc.sbuf_tensor(f"fout{i}", [128, DT, 128], BF16))
                    for i in range(2)]
            xnT = e2(nc.sbuf_tensor("xnT", [128, DT, S], BF16, is_transpose_dest=True))
            beffTb = e2(nc.sbuf_tensor("beffTb", [R, 3, D], BF16))
            qs_sl = e2(nc.sbuf_tensor("qs_sl", [128, DT, 128], BF16))
            ks_sl = e2(nc.sbuf_tensor("ks_sl", [128, DT, 128], BF16))
            vs_sl = e2(nc.sbuf_tensor("vs_sl", [128, DT, 128], BF16))
            ropeC = e2(nc.sbuf_tensor("ropeC", [128, S], BF16))
            ropeS = e2(nc.sbuf_tensor("ropeS", [128, S], BF16))
            masks = e2(nc.sbuf_tensor("masks", [128, 4, 512], BF16))
            qraw = e2(nc.sbuf_tensor("qraw", [128, 512], F32))
            tmp1 = e2(nc.sbuf_tensor("tmp1", [128, 512], F32))
            qhT = e2(nc.sbuf_tensor("qhT", [128, S], BF16))
            khT = e2(nc.sbuf_tensor("khT", [128, S], BF16))
            vhT = e2(nc.sbuf_tensor("vhT", [128, 512], F32))
            vh = e2(nc.sbuf_tensor("vh", [128, DT, 128], BF16))
            pT = e2(nc.sbuf_tensor("pT", [128, 2, 512], BF16))
            rd_sb = e2(nc.sbuf_tensor("rd_sb", [1, 512], F32))
            rbc = e2(nc.sbuf_tensor("rbc", [128, 512], F32))
            lowT = e2(nc.sbuf_tensor("lowT", [R, 4 * S], BF16))
            ahT = e2(nc.sbuf_tensor("ahT", [128, S], BF16))
            lg_sb = e2(nc.sbuf_tensor("lg_sb", [128, E], F32))
            lgst = e2(nc.sbuf_tensor("lgst", [128, 2], F32))
            probs = e2(nc.sbuf_tensor("probs", [128, E], F32))
            top8 = e2(nc.sbuf_tensor("top8", [128, 8], F32))
            combst = e2(nc.sbuf_tensor("combst", [E, 128], F32))
            aT = e2(nc.sbuf_tensor("aT", [128, 2, 512], BF16))
            wost = e2(nc.sbuf_tensor("wost", [128, 2, 512], F32))
            wo_blk = e2(nc.sbuf_tensor("wo_blk", [128, 4, DT, 128], BF16))
            psA = e2(nc.psum_tensor("psA", [128, 2048], F32))
            psB = e2(nc.psum_tensor("psB", [128, 512], F32))
            psC = e2(nc.psum_tensor("psC", [128, 512], F32))
            psD = e2(nc.psum_tensor("psD", [128, 512], F32))
            psE = e2(nc.psum_tensor("psE", [128, 512], F32))

            cr = [sch.dma("s", ropeC[:, :], ropeC_in[:, :],
                          waits=[lora_ok_v, lora_ok_t], sem=sem("rope")),
                  sch.dma("s", ropeS[:, :], ropeS_in[:, :], sem=sem("rope")),
                  sch.dma("s", masks[:, :, :], mask_in[:, :, :], sem=sem("rope"))]
            consts_ok = cr

            fblocks = []
            for p in range(4):
                for hb in range(H):
                    fblocks.append((W_in[p][hb * 128:(hb + 1) * 128, :],
                                    wT_sc[p][hb], ("w", p, hb)))
            for e_ in range(E):
                for ob in range(22):
                    fblocks.append((w1_in[e_][ob * 128:(ob + 1) * 128, :],
                                    w1T_sc[e_, ob], ("w1", e_, ob)))
                for it in range(I_MOE // 128):
                    fblocks.append((
                        w2_in[e_][:, it * 128:(it + 1) * 128].rearrange(
                            "(hb pp) c -> pp hb c", pp=128),
                        w2T_sc[e_, it], ("w2", e_, it)))
            for ob in range(44):
                fblocks.append((s1_in[ob * 128:(ob + 1) * 128, :],
                                s1T_sc[ob], ("s1", ob)))
            for it in range(IS // 128):
                fblocks.append((
                    s2_in[:, it * 128:(it + 1) * 128].rearrange(
                        "(hb pp) c -> pp hb c", pp=128),
                    s2T_sc[it], ("s2", it)))

            fact_tok = {}
            fstate = dict(i=0, fin_free=[None] * NFIN, fout_free=[None, None],
                          ps_free=[None, None], fin_i=0, fout_i=0)
            fr = {k: None for k in ("psAh0", "psAh1", "psB", "psC", "psD", "psE",
                                    "pT0", "pT1", "vhT", "vh", "rbc", "rd",
                                    "xnT", "wost0", "wost1", "wo_blk",
                                    "qs", "ks", "vs",
                                    "ahT", "beffTb", "att0", "att1",
                                    "aT0", "aT1", "combst")}

            def fact_emit(n):
                first = True
                while n > 0 and fstate["i"] < len(fblocks):
                    n -= 1
                    src, dst, key = fblocks[fstate["i"]]
                    fstate["i"] += 1
                    si = fstate["fin_i"] % NFIN
                    fstate["fin_i"] += 1
                    so = fstate["fout_i"] % 2
                    fstate["fout_i"] += 1
                    src3 = src if len(src.shape) == 3 else src.rearrange(
                        "p (dt c) -> p dt c", c=128)
                    cw = [fstate["fin_free"][si]]
                    if fstate["fin_i"] <= NFIN:
                        cw += [lora_ok_v, lora_ok_t]
                    cast = sch.dma("g", fin[si][:, :, :], src3,
                                   waits=cw, sem=sem("fin%d" % si))
                    evs = []
                    glast = None
                    for j in range(4):
                        pj = j % 2
                        for k in range(4):
                            dt = 4 * j + k
                            w = []
                            if dt == 0:
                                w = [cast]
                            if k == 0:
                                w.append(fstate["ps_free"][pj])
                                if first:
                                    w += [fr["psAh0"], fr["psAh1"]]
                            glast = sch.op("t", lambda e, dt=dt, pj=pj, k=k, si=si: e.matmul(
                                psA[:, pj * 512 + k * 128:pj * 512 + (k + 1) * 128],
                                fin[si][:, dt, :], idenb[:, :],
                                start=True, stop=True),
                                waits=w, inc=(tP if k == 3 else None))
                        eng = "v" if (fstate["fout_i"] + j) % 2 == 0 else "s"
                        def _ev(e, j=j, pj=pj, so=so):
                            dst_ = fout[so][:, 4 * j:4 * (j + 1), :]
                            src_ = psA[:, pj * 512:(pj + 1) * 512].rearrange(
                                "p (a c) -> p a c", c=128)
                            return (e.tensor_copy(dst_, src_)
                                    if hasattr(e, "tensor_copy") else e.copy(dst_, src_))
                        ev = sch.op(eng, _ev,
                            waits=[glast, fstate["fout_free"][so]],
                            inc=(vP if eng == "v" else sP))
                        fstate["ps_free"][pj] = ev
                        evs.append(ev)
                    first = False
                    fstate["fin_free"][si] = glast
                    st = sch.dma("s", dst, fout[so][:, :, :], waits=evs,
                                 sem=sem("fout%d" % so))
                    fstate["fout_free"][so] = st
                    fact_tok[key] = st
                if not first:
                    fr["psAh0"] = fstate["ps_free"][0]
                    fr["psAh1"] = fstate["ps_free"][1]

            fact_emit(4 * H)

            for b in range(B):
                brow = b * S
                xnT_toks = []
                for dt in range(DT):
                    xnT_toks.append(sch.dma(
                        "y", xnT[:, dt, :],
                        xn_sc[brow:brow + S, dt * 128:(dt + 1) * 128],
                        waits=[p1_done, fr["xnT"], lora_ok_v, lora_ok_t]
                        if dt == 0 else [], sem=sem("xnT"), transpose=True))
                xnT_ok = xnT_toks[-1]
                bt = sch.dma("g", beffTb[:, :, :],
                             beff_sc.rearrange("(p bb) r d -> bb r p d", bb=B)[b][:, 0:3, :],
                             waits=[beff_ok, fr["beffTb"]], sem=sem("bt"))

                for tt in range(S // 128):
                    g_t = brow + tt * 128
                    mm = None
                    for dt in range(DT):
                        mm = sch.op("t", lambda e, dt=dt, tt=tt: e.matmul(
                            psC[:, 0:E], xnT[:, dt, tt * 128:(tt + 1) * 128],
                            gate_wT[:, dt, :], start=(dt == 0), stop=(dt == DT - 1)),
                            waits=[xnT_ok, gateT_ok, fr["psC"]] if dt == 0 else [],
                            inc=(tP if dt == DT - 1 else None))
                    r1 = sch.op("v", lambda e: e.tensor_copy(lg_sb[:, :], psC[:, 0:E]),
                                waits=[mm], inc=vP)
                    r2 = sch.op("v", lambda e: e.reduce_max(lgst[:, 0:1], lg_sb[:, :],
                                                            axis=AX.X), waits=[r1], inc=vP)
                    r3 = sch.op("v", lambda e: e.tensor_scalar(
                        lgst[:, 0:1], lgst[:, 0:1], -1.0, None, op0=ALU.mult),
                        waits=[r2], inc=vP)
                    r4 = sch.op("s", lambda e: e.activation(
                        probs[:, :], lg_sb[:, :], AF.Exp, bias=lgst[:, 0:1], scale=1.0),
                        waits=[r3], inc=sP)
                    r5 = sch.op("v", lambda e: e.reduce_sum(lgst[:, 1:2], probs[:, :],
                                                            axis=AX.X), waits=[r4], inc=vP)
                    r6 = sch.op("v", lambda e: e.reciprocal(lgst[:, 1:2], lgst[:, 1:2]),
                                waits=[r5], inc=vP)
                    r7 = sch.op("v", lambda e: e.tensor_scalar(
                        probs[:, :], probs[:, :], lgst[:, 1:2], None, op0=ALU.mult),
                        waits=[r6], inc=vP)
                    r8 = sch.op("v", lambda e: e.max(out=top8[:, :], in_=probs[:, :]),
                                waits=[r7], inc=vP)
                    r9 = sch.op("v", lambda e: e.tensor_scalar(
                        lg_sb[:, :], probs[:, :], top8[:, 5:6], None, op0=ALU.is_ge),
                        waits=[r8, r1], inc=vP)
                    r10 = sch.op("v", lambda e: e.tensor_mul(
                        lg_sb[:, :], lg_sb[:, :], probs[:, :]), waits=[r9], inc=vP)
                    r11 = sch.op("t", lambda e: e.transpose(
                        psC[0:E, 0:128], lg_sb[:, :], iden[:, :]),
                        waits=[r10], inc=tP)
                    r12 = sch.op("v", lambda e: e.tensor_copy(
                        combst[:, :], psC[0:E, 0:128]),
                        waits=[r11, fr["combst"]], inc=vP)
                    fr["psC"] = r12
                    fr["combst"] = sch.dma("s", comb_sc[:, g_t:g_t + 128],
                                           combst[:, :], waits=[r12], sem=sem("comb"))

                for p in range(3):
                    pb = p * B + b
                    for t4 in range(S // 512):
                        mm = None
                        for dt in range(DT):
                            mm = sch.op("t", lambda e, pb=pb, dt=dt, t4=t4: e.matmul(
                                psB[0:R, :], aeffT[:, pb * DT + dt, :],
                                xnT[:, dt, t4 * 512:(t4 + 1) * 512],
                                start=(dt == 0), stop=(dt == DT - 1)),
                                waits=[xnT_ok, fr["psB"]] if dt == 0 else [],
                                inc=(tP if dt == DT - 1 else None))
                        ev = sch.op("v", lambda e, p=p, t4=t4: e.tensor_copy(
                            lowT[:, p * S + t4 * 512:p * S + (t4 + 1) * 512],
                            psB[0:R, :]), waits=[mm], inc=vP)
                        fr["psB"] = ev
                low_ok = vP.tok()

                att_store = []
                for h in range(H):
                    sq = sch.dma("y", qs_sl[:, :, :], wT_sc[0][h],
                                 waits=[fact_tok[("w", 0, h)], fr["qs"]], sem=sem("qs"))
                    sk = sch.dma("y", ks_sl[:, :, :], wT_sc[1][h],
                                 waits=[fact_tok[("w", 1, h)], fr["ks"]], sem=sem("ks"))
                    sv = sch.dma("y", vs_sl[:, :, :], wT_sc[2][h],
                                 waits=[fact_tok[("w", 2, h)], fr["vs"]], sem=sem("vs"))

                    def proj(slab, slab_tok, pidx, out_bf, roped):
                        pb = pidx * B + b
                        mm_last = []
                        for t4 in range(S // 512):
                            qsl = slice(t4 * 512, (t4 + 1) * 512)
                            for dt in range(DT):
                                sch.op("t", lambda e, dt=dt, qsl=qsl, slab=slab: e.matmul(
                                    psA[:, qsl], slab[:, dt, :], xnT[:, dt, qsl],
                                    start=(dt == 0), stop=False),
                                    waits=[slab_tok, xnT_ok, fr["psAh0"],
                                           fr["psAh1"]]
                                    if (dt == 0 and t4 == 0) else [])
                            mm_last.append(sch.op(
                                "t", lambda e, qsl=qsl, pidx=pidx, t4=t4, h=h: e.matmul(
                                    psA[:, qsl],
                                    beffTb[:, pidx, h * 128:(h + 1) * 128],
                                    lowT[:, pidx * S + t4 * 512:pidx * S + (t4 + 1) * 512],
                                    start=False, stop=True),
                                waits=[low_ok, bt] if t4 == 0 else [],
                                inc=tP))
                        outs = []
                        for t4 in range(S // 512):
                            qsl = slice(t4 * 512, (t4 + 1) * 512)
                            if not roped:
                                ev = sch.op("v", lambda e, qsl=qsl: e.tensor_copy(
                                    vhT[:, :], psA[:, qsl]),
                                    waits=[mm_last[t4], fr["vhT"]], inc=vP)
                                for c4 in range(4):
                                    kt = t4 * 4 + c4
                                    tt_ = sch.op("t", lambda e, c4=c4: e.transpose(
                                        psE[:, 0:128], vhT[:, c4 * 128:(c4 + 1) * 128],
                                        iden[:, :]),
                                        waits=[ev, fr["psE"]], inc=tP)
                                    vtr = sch.op("v", lambda e, kt=kt: e.tensor_copy(
                                        vh[:, kt, :], psE[:, 0:128]),
                                        waits=[tt_, fr["vh"]], inc=vP)
                                    fr["psE"] = vtr
                                    outs.append(vtr)
                                fr["vhT"] = tt_
                                continue
                            ev = sch.op("v", lambda e, qsl=qsl: e.tensor_copy(
                                qraw[:, :], psA[:, qsl]),
                                waits=[mm_last[t4],
                                       outs[-1] if outs else fr.get("qraw")],
                                inc=vP)
                            pm = sch.op("t", lambda e: e.matmul(
                                psB[:, :], perm[:, :], qraw[:, :],
                                start=True, stop=True),
                                waits=[ev, fr["psB"]] + consts_ok, inc=tP)
                            w1_ = sch.op("v", lambda e, qsl=qsl: e.tensor_mul(
                                tmp1[:, :], ropeS[:, qsl], psB[:, :]),
                                waits=[pm], inc=vP)
                            fr["psB"] = w1_
                            w2_ = sch.op("v", lambda e, qsl=qsl: e.tensor_mul(
                                qraw[:, :], ropeC[:, qsl], qraw[:, :]),
                                waits=[w1_], inc=vP)
                            outs.append(sch.op(
                                "v", lambda e, qsl=qsl, o=out_bf: e.tensor_add(
                                    o[:, qsl], qraw[:, :], tmp1[:, :]),
                                waits=[w2_], inc=vP))
                        fr["psAh0"] = outs[-1]
                        fr["psAh1"] = outs[-1]
                        if roped:
                            fr["qraw"] = outs[-1]
                        return outs[-1]

                    q_ok = proj(qs_sl, sq, 0, qhT, True)
                    fr["qs"] = tP.tok()
                    if dbg and b == 0 and h == 0:
                        sch.dma("g", dbg_outs["o_qh"][:, :], qhT[:, :],
                                waits=[q_ok], sem=sem("dbg"))
                    k_ok = proj(ks_sl, sk, 1, khT, True)
                    fr["ks"] = tP.tok()
                    v_ok = proj(vs_sl, sv, 2, None, False)
                    v_ready = v_ok
                    fr["vs"] = tP.tok()
                    if dbg and b == 0 and h == 0:
                        sch.dma("g", dbg_outs["o_kh"][:, :], khT[:, :],
                                waits=[k_ok], sem=sem("dbg"))
                        sch.dma("g", dbg_outs["o_vh"][:, :, :], vh[:, :, :],
                                waits=[v_ok], sem=sem("dbg"))

                    for qt in range(4):
                        qsl = slice(qt * 512, (qt + 1) * 512)
                        KT = 4 * qt + 4
                        dn_last = None
                        for kt in range(KT):
                            sp_ = kt % 2
                            ps_sc = psB if sp_ == 0 else psC
                            key_sc = "psB" if sp_ == 0 else "psC"
                            sc = sch.op("t", lambda e, kt=kt, qsl=qsl, ps=ps_sc: e.matmul(
                                ps[:, :], khT[:, kt * 128:(kt + 1) * 128],
                                qhT[:, qsl], start=True, stop=True),
                                waits=[q_ok, k_ok, fr[key_sc]], inc=tP)
                            ex = sch.op("s", lambda e, sp_=sp_, ps=ps_sc: e.activation(
                                pT[:, sp_, :], ps[:, :], AF.Exp,
                                bias=neg8[:, 0:1], scale=float(SCALE)),
                                waits=[sc, fr["pT%d" % sp_], neg8_ok], inc=sP)
                            fr[key_sc] = ex
                            pt_ready = ex
                            if kt >= 4 * qt:
                                pt_ready = sch.op("v", lambda e, sp_=sp_, mi=kt - 4 * qt: e.tensor_mul(
                                    pT[:, sp_, :], pT[:, sp_, :], masks[:, mi, :]),
                                    waits=[ex] + consts_ok, inc=vP)
                            sch.op("t", lambda e, kt=kt, sp_=sp_, KT=KT: e.matmul(
                                psD[:, :], vh[:, kt, :], pT[:, sp_, :],
                                start=(kt == 0), stop=(kt == KT - 1)),
                                waits=[pt_ready, v_ready] +
                                      ([fr["psD"]] if kt == 0 else []))
                            dn_last = sch.op("t", lambda e, kt=kt, sp_=sp_, KT=KT: e.matmul(
                                psE[0:1, :], onescol[:, :], pT[:, sp_, :],
                                start=(kt == 0), stop=(kt == KT - 1)),
                                waits=([fr["psE"]] if kt == 0 else []),
                                inc=tP)
                            fr["pT%d" % sp_] = dn_last
                        rc = sch.op("v", lambda e: e.reciprocal(rd_sb[:, :], psE[0:1, :]),
                                    waits=[dn_last, fr["rd"]], inc=vP)
                        fr["psE"] = rc
                        bc = sch.op("t", lambda e: e.matmul(
                            psB[:, :], ones128[:, :], rd_sb[:, :], start=True, stop=True),
                            waits=[rc, fr["psB"]], inc=tP)
                        fr["rd"] = bc
                        bv = sch.op("v", lambda e: e.tensor_copy(rbc[:, :], psB[:, :]),
                                    waits=[bc, fr["rbc"]], inc=vP)
                        fr["psB"] = bv
                        at = sch.op("v", lambda e, qsl=qsl: e.tensor_mul(
                            ahT[:, qsl], psD[:, :], rbc[:, :]),
                            waits=[bv, fr["ahT"]], inc=vP)
                        fr["psD"] = at
                        fr["rbc"] = at
                    ast = sch.dma("s", attnT_sc[b, h], ahT[:, :], waits=[at],
                                  sem=sem("ast"))
                    fr["ahT"] = ast
                    att_store.append(ast)
                    fact_emit(12)
                attn_ok = att_store[-1]
                fr["xnT"] = tP.tok()
                fr["beffTb"] = tP.tok()

                pb_o = 3 * B + b
                for t4 in range(S // 512):
                    t4sl = slice(t4 * 512, (t4 + 1) * 512)
                    mm = None
                    for dt in range(DT):
                        apar = dt % 2
                        lld = sch.dma(
                            "s", aT[:, apar, :],
                            attnT_sc[b, dt][:, t4sl],
                            waits=[attn_ok, fr["aT%d" % apar]],
                            sem=sem("aT%d" % apar))
                        mm = sch.op("t", lambda e, dt=dt, t4=t4, pb_o=pb_o, apar=apar: e.matmul(
                            psB[0:R, :], aeffT[:, pb_o * DT + dt, :],
                            aT[:, apar, :],
                            start=(dt == 0), stop=(dt == DT - 1)),
                            waits=[lld] + ([fr["psB"]] if dt == 0 else []),
                            inc=tP)
                        fr["aT%d" % apar] = mm
                    ev = sch.op("v", lambda e, t4=t4: e.tensor_copy(
                        lowT[:, 3 * S + t4 * 512:3 * S + (t4 + 1) * 512],
                        psB[0:R, :]), waits=[mm], inc=vP)
                    fr["psB"] = ev
                lowo_ok = vP.tok()
                bt_o = sch.dma("g", beffTb[:, 0, :], beff_sc[3 * B + b],
                               waits=[beff_ok, fr["beffTb"]], sem=sem("bt"))
                fr["beffTb"] = None

                for og in range(4):
                    wg = None
                    for i in range(4):
                        wg = sch.dma("y", wo_blk[:, i, :, :], wT_sc[3][og * 4 + i],
                                     waits=[fact_tok[("w", 3, og * 4 + i)],
                                            fr["wo_blk"] if i == 0 else None],
                                     sem=sem("wo"))
                    wo_ok = wg
                    for t4 in range(S // 512):
                        t4sl = slice(t4 * 512, (t4 + 1) * 512)
                        for dt in range(DT):
                            apar = dt % 2
                            lld = sch.dma(
                                "s", aT[:, apar, :],
                                attnT_sc[b, dt][:, t4sl],
                                waits=[fr["aT%d" % apar]], sem=sem("aT%d" % apar))
                            for i in range(4):
                                w = []
                                if dt == 0 and i == 0:
                                    w = [wo_ok, fr["psAh0"], fr["psAh1"]]
                                mmw = sch.op("t", lambda e, dt=dt, i=i, apar=apar: e.matmul(
                                    psA[:, i * 512:(i + 1) * 512],
                                    wo_blk[:, i, dt, :], aT[:, apar, :],
                                    start=(dt == 0), stop=False),
                                    waits=[lld] + w,
                                    inc=(tP if i == 3 else None))
                            fr["aT%d" % apar] = mmw
                        lora_last = None
                        for i in range(4):
                            lora_last = sch.op("t", lambda e, i=i, og=og, t4=t4: e.matmul(
                                psA[:, i * 512:(i + 1) * 512],
                                beffTb[:, 0, (og * 4 + i) * 128:(og * 4 + i + 1) * 128],
                                lowT[:, 3 * S + t4 * 512:3 * S + (t4 + 1) * 512],
                                start=False, stop=True),
                                waits=[lowo_ok, bt_o],
                                inc=(tP if i == 3 else None))
                        ev = None
                        for i in range(4):
                            wpar = i % 2
                            ev = sch.op("v", lambda e, i=i, wpar=wpar: e.tensor_copy(
                                wost[:, wpar, :], psA[:, i * 512:(i + 1) * 512]),
                                waits=[lora_last, fr["wost%d" % wpar]], inc=vP)
                            st = sch.dma("s",
                                         attn_sc[og * 4 + i][:, brow + t4 * 512:
                                                             brow + (t4 + 1) * 512],
                                         wost[:, wpar, :], waits=[ev],
                                         sem=sem("wst%d" % wpar))
                            fr["wost%d" % wpar] = st
                        fr["psAh0"] = ev
                        fr["psAh1"] = ev
                    fr["wo_blk"] = tP.tok()
                fr["beffTb"] = tP.tok()
            fact_emit(10 ** 9)
            attn_all = sem("ast").tok()
            wst_all = [sem("wst0").tok(), sem("wst1").tok()]
            comb_all = vP.tok()
            comb_all_dA = sem("comb").tok()
            p2_t = tP.tok()
            p2_v = vP.tok()
            p2_s = sP.tok()
            p2_dA = wst_all + [attn_all, comb_all_dA,
                               sem("fout0").tok(), sem("fout1").tok()]
            if dbg:
                sch.dma("g", dbg_outs["o_attn"][:, :, :], attn_sc[:, :, :],
                        waits=wst_all, sem=sem("dbg"))
                sch.dma("g", dbg_outs["o_ah"][:, :, :, :], attnT_sc[:, :, :, :],
                        waits=[attn_all], sem=sem("dbg"))
                sch.dma("g", dbg_outs["o_comb"][:, :], comb_sc[:, :],
                        waits=[comb_all_dA], sem=sem("dbg"))

        with ExitStack() as p3:
            e3 = p3.enter_context
            xnB = e3(nc.sbuf_tensor("xnB", [128, DT, TBLK], BF16, is_transpose_dest=True))
            acc = e3(nc.sbuf_tensor("acc", [128, TBLK // 128, D], F32))
            act = e3(nc.sbuf_tensor("act", [128, 11, TBLK], BF16))
            w1b = e3(nc.sbuf_tensor("w1b", [128, 2, 2, DT, 128], BF16))
            w2s = e3(nc.sbuf_tensor("w2s", [128, 2, 11, 512], BF16))
            cbs = e3(nc.sbuf_tensor("cbs", [128, TBLK], F32))
            sg = e3(nc.sbuf_tensor("sg", [128, 2, 512], BF16))
            ucb = e3(nc.sbuf_tensor("ucb", [128, 2, 512], BF16))
            mrg = e3(nc.sbuf_tensor("mrg", [128, 256], F32))
            eall = e3(nc.sbuf_tensor("eall", [E, 2, 128], F32))
            combB = e3(nc.sbuf_tensor("combB", [E, 2, 512], F32))
            att3 = e3(nc.sbuf_tensor("att3", [128, 2, 4, 128], F32))
            psA3 = e3(nc.psum_tensor("psA3", [128, 2048], F32))
            psB3 = e3(nc.psum_tensor("psB3", [128, 512], F32))
            psC3 = e3(nc.psum_tensor("psC3", [128, 512], F32))
            psD3 = e3(nc.psum_tensor("psD3", [128, 512], F32))
            psE3 = e3(nc.psum_tensor("psE3", [128, 512], F32))
            GU = [((psB3, "B"), (psC3, "C")), ((psD3, "D"), (psE3, "E"))]

            bar = [p2_t, p2_v, p2_s] + p2_dA
            if dbg:
                bar.append(sem("dbg").tok())

            fr3 = {k: None for k in ("B", "C", "D", "E", "dn0", "dn1",
                                     "xnB", "w2s", "sg0", "sg1", "uc0", "uc1",
                                     "cbs", "mrg", "slab0", "slab1", "act",
                                     "combB", "w2s0", "w2s1", "att30", "att31")}
            out_store = [[None] * (TBLK // 128) for _ in range(NBLK)]
            experts = [("moe", i) for i in range(E)] + [("sh", 0), ("sh", 1)]

            for blk in range(NBLK):
                t0 = blk * TBLK
                xnB_toks = []
                for dt in range(DT):
                    xnB_toks.append(sch.dma(
                        "y", xnB[:, dt, :],
                        xn_sc[t0:t0 + TBLK, dt * 128:(dt + 1) * 128],
                        waits=([p1_done, fr3["xnB"]] + bar) if dt == 0 else [],
                        sem=sem("xnB"), transpose=True))
                xnB_ok = xnB_toks[-1]
                combB_tok = sch.dma("g", combB[:, :, :],
                                    comb_sc[:, t0:t0 + TBLK].rearrange(
                                        "r (a c) -> r a c", c=512),
                                    waits=[comb_all_dA, fr3["combB"]] + bar,
                                    sem=sem("cB"))
                acc_last = [None] * (TBLK // 128)

                eall_free = bar[0] if False else None
                for ei, (kind, eidx) in enumerate(experts):
                    if kind == "moe":
                        ep = eidx % 2
                        eall_tok = sch.dma(
                            "g", eall[:, ep, :],
                            eall_in[:, eidx * 128:(eidx + 1) * 128],
                            waits=bar + [fr3.get("eall%d" % ep)],
                            sem=sem("ea%d" % ep))
                        cb_mm0 = sch.op("t", lambda e, eidx=eidx, ep=ep: e.matmul(
                            psB3[:, :], eall[:, ep, :],
                            combB[:, 0, :], start=True, stop=True),
                            waits=[combB_tok, eall_tok, fr3["B"]], inc=tP)
                        cb_mm1 = sch.op("t", lambda e, eidx=eidx, ep=ep: e.matmul(
                            psC3[:, :], eall[:, ep, :],
                            combB[:, 1, :], start=True, stop=True),
                            waits=[fr3["C"]], inc=tP)
                        fr3["eall%d" % ep] = cb_mm1
                        cb0 = sch.op("v", lambda e: e.tensor_copy(
                            cbs[:, 0:512], psB3[:, :]), waits=[cb_mm0, fr3["cbs"]],
                            inc=vP)
                        cb1 = sch.op("v", lambda e: e.tensor_copy(
                            cbs[:, 512:1024], psC3[:, :]), waits=[cb_mm1], inc=vP)
                        fr3["B"] = cb0
                        fr3["C"] = cb1
                        cb_ok = [cb0, cb1]
                    else:
                        cb_ok = []

                    act_toks = []
                    for icg in range(11):
                        par = icg % 2
                        if kind == "moe":
                            gblk = w1T_sc[eidx, icg]
                            ublk = w1T_sc[eidx, 11 + icg]
                            gtk = fact_tok[("w1", eidx, icg)]
                            utk = fact_tok[("w1", eidx, 11 + icg)]
                        else:
                            gblk = s1T_sc[eidx * 11 + icg]
                            ublk = s1T_sc[22 + eidx * 11 + icg]
                            gtk = fact_tok[("s1", eidx * 11 + icg)]
                            utk = fact_tok[("s1", 22 + eidx * 11 + icg)]
                        gl = sch.dma("y", w1b[:, par, 0, :, :], gblk,
                                     waits=[gtk, fr3["slab%d" % par]] + bar,
                                     sem=sem("w1b%d" % par))
                        ul = sch.dma("y", w1b[:, par, 1, :, :], ublk,
                                     waits=[utk], sem=sem("w1b%d" % par))
                        for tsub in range(2):
                            (ps_g, kg), (ps_u, ku) = GU[tsub]
                            tsl = slice(tsub * 512, (tsub + 1) * 512)
                            g_last = None
                            for dt in range(DT):
                                w = [ul, xnB_ok, fr3[kg]] if dt == 0 else []
                                g_last = sch.op("t", lambda e, dt=dt, par=par, tsl=tsl, ps=ps_g: e.matmul(
                                    ps[:, :], w1b[:, par, 0, dt, :],
                                    xnB[:, dt, tsl], start=(dt == 0), stop=(dt == DT - 1)),
                                    waits=w, inc=(tP if dt == DT - 1 else None))
                            u_last = None
                            for dt in range(DT):
                                w = [ul, fr3[ku]] if dt == 0 else []
                                u_last = sch.op("t", lambda e, dt=dt, par=par, tsl=tsl, ps=ps_u: e.matmul(
                                    ps[:, :], w1b[:, par, 1, dt, :],
                                    xnB[:, dt, tsl], start=(dt == 0), stop=(dt == DT - 1)),
                                    waits=w, inc=(tP if dt == DT - 1 else None))
                            sgt = sch.op("s", lambda e, tsub=tsub, ps=ps_g: e.activation(
                                sg[:, tsub, :], ps[:, :], AF.Silu),
                                waits=[g_last, fr3["sg%d" % tsub]], inc=sP)
                            fr3[kg] = sgt
                            if kind == "moe":
                                uct = sch.op("v", lambda e, tsub=tsub, tsl=tsl, ps=ps_u: e.tensor_mul(
                                    ucb[:, tsub, :], ps[:, :], cbs[:, tsl]),
                                    waits=[u_last, fr3["uc%d" % tsub]] + cb_ok, inc=vP)
                            else:
                                uct = sch.op("v", lambda e, tsub=tsub, ps=ps_u: e.tensor_copy(
                                    ucb[:, tsub, :], ps[:, :]),
                                    waits=[u_last, fr3["uc%d" % tsub]], inc=vP)
                            fr3[ku] = uct
                            at3 = sch.op("v", lambda e, tsub=tsub, icg=icg, tsl=tsl: e.tensor_mul(
                                act[:, icg, tsl], sg[:, tsub, :], ucb[:, tsub, :]),
                                waits=[sgt, uct, fr3["act"]], inc=vP)
                            fr3["sg%d" % tsub] = at3
                            fr3["uc%d" % tsub] = at3
                            act_toks.append(at3)
                        fr3["slab%d" % par] = tP.tok()
                    act_ok = act_toks[-1]
                    if kind == "moe":
                        fr3["cbs"] = act_toks[-1]

                    for hq in range(4):
                        wp2 = (ei * 4 + hq) % 2
                        w2_toks = []
                        for it in range(11):
                            if kind == "moe":
                                src = w2T_sc[eidx, it][:, hq * 4:(hq + 1) * 4, :]
                                wtok = fact_tok[("w2", eidx, it)]
                            else:
                                src = s2T_sc[eidx * 11 + it][:, hq * 4:(hq + 1) * 4, :]
                                wtok = fact_tok[("s2", eidx * 11 + it)]
                            w = [wtok] + (([fr3["w2s%d" % wp2]] + bar)
                                          if it == 0 else [])
                            w2_toks.append(sch.dma(
                                "y", w2s[:, wp2, it, :].rearrange("p (a c) -> p a c", c=128),
                                src, waits=w, sem=sem("w2s%d" % wp2)))
                        w2_ok = w2_toks[-1]
                        for tt in range(TBLK // 128):
                            hp = tt % 2
                            ps_dn = psA3[:, hp * 512:(hp + 1) * 512]
                            dn_last = None
                            fold_attn = (ei == 0)
                            for ic in range(11):
                                w = []
                                if ic == 0:
                                    w = [w2_ok, act_ok, fr3["dn%d" % hp]]
                                dn_last = sch.op("t", lambda e, ic=ic, tt=tt, wp2=wp2,
                                                 ps=ps_dn, fa=fold_attn: e.matmul(
                                    ps[:, :],
                                    act[:, ic, tt * 128:(tt + 1) * 128],
                                    w2s[:, wp2, ic, :],
                                    start=(ic == 0),
                                    stop=(ic == 10 and not fa)),
                                    waits=w,
                                    inc=(tP if (ic == 10 and not fold_attn) else None))
                            if fold_attn:
                                glob_t = t0 + tt * 128
                                trs = slice(glob_t, glob_t + 128)
                                a3p = tt % 2
                                a3ld = []
                                for c in range(4):
                                    a3ld.append(sch.dma(
                                        "g", att3[:, a3p, c, :],
                                        attn_sc[hq * 4 + c][:, trs],
                                        waits=(bar + [fr3["att3%d" % a3p]])
                                        if c == 0 else [], sem=sem("a3%d" % a3p)))
                                for c in range(4):
                                    dn_last = sch.op("t", lambda e, c=c, a3p=a3p, ps=ps_dn: e.matmul(
                                        ps[:, c * 128:(c + 1) * 128],
                                        att3[:, a3p, c, :], iden[:, :],
                                        start=False, stop=(c == 3)),
                                        waits=[a3ld[3]],
                                        inc=(tP if c == 3 else None))
                                fr3["att3%d" % a3p] = dn_last
                            hsl = slice(hq * 512, (hq + 1) * 512)
                            if ei == 0:
                                ev = sch.op("v", lambda e, tt=tt, hsl=hsl, ps=ps_dn: e.tensor_copy(
                                    acc[:, tt, hsl], ps[:, :]),
                                    waits=[dn_last,
                                           out_store[blk - 1][-1] if blk else None],
                                    inc=vP)
                            else:
                                ev = sch.op("v", lambda e, tt=tt, hsl=hsl, ps=ps_dn: e.tensor_add(
                                    acc[:, tt, hsl], acc[:, tt, hsl], ps[:, :]),
                                    waits=[dn_last, acc_last[tt]], inc=vP)
                            fr3["dn%d" % hp] = ev
                            acc_last[tt] = ev
                        fr3["w2s%d" % wp2] = tP.tok()
                    fr3["act"] = tP.tok()
                fr3["xnB"] = tP.tok()
                fr3["combB"] = tP.tok()

                for tt in range(TBLK // 128):
                    rows = slice(t0 + tt * 128, t0 + (tt + 1) * 128)
                    mtok = None
                    for half in range(8):
                        csl = slice(half * 256, (half + 1) * 256)
                        lx = sch.dma("s", mrg[:, :], x_flat[rows, csl],
                                     waits=[fr3["mrg"], acc_last[tt]], sem=sem("mrg"))
                        mtok = sch.op("v", lambda e, tt=tt, csl=csl: e.tensor_add(
                            acc[:, tt, csl], acc[:, tt, csl], mrg[:, :]),
                            waits=[lx], inc=vP)
                        fr3["mrg"] = mtok
                    st = sch.dma("s", out_flat[rows, :], acc[:, tt, :],
                                 waits=[mtok], sem=sem("out"))
                    out_store[blk][tt] = st
                    acc_last[tt] = None

        finals = [sem("out").tok()]
        if dbg:
            finals.append(sem("dbg").tok())
        sch.op("g", lambda e: e.engine_nop(), waits=finals)

        with nc.Block() as block:
            sch.emit(block)

    return nc, dbg_outs


def _host_constants():
    import ml_dtypes
    iden = np.eye(128, dtype=np.float32)
    idenb = np.eye(128, dtype=np.float32).astype(ml_dtypes.bfloat16)
    perm = np.zeros((128, 128), dtype=np.float32)
    for i in range(64):
        perm[2 * i, 2 * i + 1] = 1.0
        perm[2 * i + 1, 2 * i] = 1.0
    pos = np.arange(S, dtype=np.float64)
    invf = THETA ** (-np.arange(0, DH, 2, dtype=np.float64) / DH)
    ang = invf[:, None] * pos[None, :]
    C = np.empty((128, S), dtype=np.float32)
    Sg = np.empty((128, S), dtype=np.float32)
    C[0::2, :] = np.cos(ang)
    C[1::2, :] = np.cos(ang)
    Sg[0::2, :] = -np.sin(ang)
    Sg[1::2, :] = np.sin(ang)
    masks = np.zeros((128, 4, 512), dtype=np.float32)
    for r_ in range(4):
        for kp in range(128):
            q0 = 128 * r_ + kp
            if q0 < 512:
                masks[kp, r_, q0:] = 1.0
    eall = np.zeros((E, E * 128), dtype=np.float32)
    for e_ in range(E):
        eall[e_, e_ * 128:(e_ + 1) * 128] = 1.0
    return dict(
        c_iden=iden, c_idenb=np.asarray(idenb), c_perm=perm,
        c_ropec=np.asarray(C.astype(ml_dtypes.bfloat16)),
        c_ropes=np.asarray(Sg.astype(ml_dtypes.bfloat16)),
        c_mask=np.asarray(masks.astype(ml_dtypes.bfloat16)),
        c_eall=eall,
        c_ones16=np.ones((1, 16), dtype=np.float32),
        c_ones128=np.ones((1, 128), dtype=np.float32),
        c_onescol=np.asarray(np.ones((128, 1), dtype=np.float32).astype(ml_dtypes.bfloat16)),
        c_iota16=np.arange(16, dtype=np.float32).reshape(16, 1),
        c_iota4r=np.broadcast_to(np.arange(NL, dtype=np.float32), (16, NL)).copy(),
    )


def _make_in_maps(inputs, consts):
    x = np.ascontiguousarray(np.asarray(inputs["L0_attn_x"], dtype=np.float32))
    starts = np.ascontiguousarray(
        np.asarray(inputs["L0_attn_lora_start_indices"], dtype=np.int32).reshape(1, NL))
    weights = {}
    for k, v in inputs.items():
        if k in ("L0_attn_x", "L0_attn_lora_start_indices"):
            continue
        weights[k] = np.ascontiguousarray(np.asarray(v, dtype=np.float32))
    in_maps = []
    for c in range(NCORES):
        m = dict(weights)
        m["L0_attn_x"] = np.ascontiguousarray(x[c * B:(c + 1) * B])
        m["L0_attn_lora_start_indices"] = starts
        er = np.zeros((16, B * 128), dtype=np.float32)
        for i in range(B):
            er[c * B + i, i * 128:(i + 1) * 128] = 1.0
        m["c_erow"] = er
        m.update(consts)
        in_maps.append(m)
    return in_maps


def kernel(**inputs):
    nc, _ = build()
    consts = _host_constants()
    in_maps = _make_in_maps(inputs, consts)
    res = run_bass_kernel_spmd(nc, in_maps, core_ids=list(range(NCORES)))
    return np.concatenate([r["out"] for r in res.results], axis=0)


# version 04
# speedup vs baseline: 1.1418x; 1.1418x over previous
import math
from contextlib import ExitStack

import numpy as np
import concourse.bass as bass
import concourse.mybir as mybir
from concourse.bass_utils import run_bass_kernel_spmd

F32 = mybir.dt.float32
BF16 = mybir.dt.bfloat16
I32 = mybir.dt.int32
AF = mybir.ActivationFunctionType
ALU = mybir.AluOpType
AX = mybir.AxisListType

NCORES = 8
B = 2
S = 2048
D = 2048
H = 16
DH = 128
DT = D // 128
R = 16
NL = 4
E = 16
I_MOE = 1408
IS = 2816
TCORE = B * S
TBLK = 1024
NBLK = TCORE // TBLK
SCALE = 1.0 / math.sqrt(DH)
EXP_BIAS = -8.0
EPS = 1e-5
THETA = 10000.0


class Sem:
    def __init__(self, nc, name):
        self.h = nc.alloc_semaphore(name)
        self.c = 0

    def tok(self):
        return (self, self.c)


class Sched:
    def __init__(self):
        self.items = []

    def op(self, eng, fn, waits=(), inc=None, inc_n=1):
        ws = [w for w in waits if w is not None]
        for (sem, val) in ws:
            assert val <= sem.c, f"wait on future sem value {val} > {sem.c}"
        tok = None
        if inc is not None:
            inc.c += inc_n
            tok = inc.tok()
        self.items.append((eng, fn, tuple(ws), (inc, inc_n) if inc else None))
        return tok

    def dma(self, eng, out_ap, in_ap, waits=(), sem=None, transpose=False):
        if transpose:
            def fn(e, o=out_ap, i=in_ap):
                return e.dma_start(o, i, transpose=True)
        else:
            def fn(e, o=out_ap, i=in_ap):
                return e.dma_start(o, i)
        return self.op(eng, fn, waits, inc=sem, inc_n=16)

    def emit(self, block):
        items = self.items

        def mk(engname):
            def run(eng):
                waited = {}
                for (en, fn, waits, inc) in items:
                    if en != engname:
                        continue
                    agg = {}
                    for (sem, val) in waits:
                        if val > agg.get(id(sem), (None, -1))[1]:
                            agg[id(sem)] = (sem, val)
                    for (sem, val) in agg.values():
                        if val <= 0 or waited.get(id(sem), -1) >= val:
                            continue
                        eng.wait_ge(sem.h, val)
                        waited[id(sem)] = val
                    ins = fn(eng)
                    if inc is not None:
                        ins.then_inc(inc[0].h, inc[1])
            return run

        block.tensor(mk("t"))
        block.vector(mk("v"))
        block.scalar(mk("s"))
        block.gpsimd(mk("g"))
        block.sync(mk("y"))


def build(dbg=False):
    nc = bass.Bass("TRN2", target_bir_lowering=False)

    def din(name, shape, dt=F32):
        return nc.dram_tensor(name, list(shape), dt, kind="ExternalInput").ap()

    x_in = din("L0_attn_x", (B, S, D))
    starts_in = din("L0_attn_lora_start_indices", (1, NL), I32)
    W_in, A_in, B_in = {}, {}, {}
    for p, nm in enumerate("qkvo"):
        W_in[p] = din(f"L0_attn_W{nm}", (D, D))
        A_in[p] = din(f"L0_attn_A{nm}", (NL, R, D))
        B_in[p] = din(f"L0_attn_B{nm}", (NL, D, R))
    gate_in = din("L0_ffn_gate_w", (E, D))
    w1_in = din("L0_ffn_w1", (E, 2 * I_MOE, D))
    w2_in = din("L0_ffn_w2", (E, D, I_MOE))
    s1_in = din("L0_ffn_shared_w1", (2 * IS, D))
    s2_in = din("L0_ffn_shared_w2", (D, IS))

    iden_in = din("c_iden", (128, 128))
    idenb_in = din("c_idenb", (128, 128), BF16)
    onesb_in = din("c_onesb", (128, 128), BF16)
    perm_in = din("c_perm", (128, 128), BF16)
    ropeC_in = din("c_ropec", (128, S), BF16)
    ropeS_in = din("c_ropes", (128, S), BF16)
    mask_in = din("c_mask", (128, 4, 512), BF16)
    eall_in = din("c_eall", (E, E * 128))
    erow_in = din("c_erow", (16, B * 128))
    ones16_in = din("c_ones16", (1, 16))
    iota16_in = din("c_iota16", (16, 1))
    iota4r_in = din("c_iota4r", (16, NL))

    def dint(name, shape, dt=BF16):
        return nc.dram_tensor(name, list(shape), dt, kind="Internal").ap()

    wT_sc = {p: dint(f"w{nm}T_sc", (H, 128, DT, 128))
             for p, nm in enumerate("qkvo")}
    w1T_sc = dint("w1T_sc", (E, 22, 128, DT, 128))
    w2T_sc = dint("w2T_sc", (E, I_MOE // 128, 128, DT, 128))
    s1T_sc = dint("s1T_sc", (44, 128, DT, 128))
    s2T_sc = dint("s2T_sc", (IS // 128, 128, DT, 128))
    gw_sc = dint("gw_sc", (DT, E, 128))
    xn_sc = dint("xn_sc", (TCORE, D))
    beff_sc = dint("beff_sc", (4 * B, R, D))
    comb_sc = dint("comb_sc", (E, TCORE), F32)
    attnT_sc = dint("attnT_sc", (B, H, 128, S))
    attn_sc = dint("attn_sc", (DT, 128, TCORE), F32)

    out_t = nc.dram_tensor("out", [B, S, D], F32, kind="ExternalOutput").ap()
    out_flat = out_t.rearrange("b s d -> (b s) d")
    x_flat = x_in.rearrange("b s d -> (b s) d")

    sch = Sched()
    tP = Sem(nc, "tP")
    vP = Sem(nc, "vP")
    sP = Sem(nc, "sP")
    _sembank = {}

    def sem(name):
        if name not in _sembank:
            _sembank[name] = Sem(nc, name)
        return _sembank[name]

    with ExitStack() as top:
        ent = top.enter_context

        iden = ent(nc.sbuf_tensor("iden", [128, 128], F32))
        idenb = ent(nc.sbuf_tensor("idenb", [128, 128], BF16))
        onesb = ent(nc.sbuf_tensor("onesb", [128, 128], BF16))
        permb = ent(nc.sbuf_tensor("permb", [128, 128], BF16))
        ones16 = ent(nc.sbuf_tensor("ones16", [1, 16], F32))
        iota16 = ent(nc.sbuf_tensor("iota16", [16, 1], F32))
        iota4r = ent(nc.sbuf_tensor("iota4r", [16, NL], F32))
        erow = ent(nc.sbuf_tensor("erow", [16, B * 128], F32))
        gate_wT = ent(nc.sbuf_tensor("gate_wT", [128, DT, E], BF16, is_transpose_dest=True))
        aeffT = ent(nc.sbuf_tensor("aeffT", [128, B * DT, 4 * R], BF16))
        neg8 = ent(nc.sbuf_tensor("neg8", [128, 1], F32))
        PB = 2
        NFIN = 2
        fin = [ent(nc.sbuf_tensor(f"fin{i}", [128, D], BF16))
               for i in range(NFIN)]
        fout = [ent(nc.sbuf_tensor(f"fout{i}", [128, DT, 128], BF16))
                for i in range(2)]
        psA = ent(nc.psum_tensor("psA", [128, 2048], F32))

        fr = {k: None for k in (
            "psA0", "psA1", "psA2", "psA3", "psB", "psC", "psD", "psE",
            "pT0", "pT1", "pT2", "pT3", "vhT", "vh", "rbc",
            "xnT", "wost", "wo_blk",
            "ahT0", "ahT1", "beffTb", "aT0", "aT1", "combst", "qraw")}

        for sb, src in ((iden, iden_in), (idenb, idenb_in), (onesb, onesb_in),
                        (permb, perm_in), (ones16, ones16_in),
                        (iota16, iota16_in), (iota4r, iota4r_in),
                        (erow, erow_in)):
            sch.dma("g", sb[:, :], src[:, :], sem=sem("c0"))
        for dt in range(DT):
            sch.dma("g", gw_sc[dt], gate_in[:, dt * 128:(dt + 1) * 128], sem=sem("c0"))
        p0a = sem("c0").tok()
        neg8_ok = sch.op("v", lambda e: e.memset(neg8[:, :], float(EXP_BIAS)), inc=vP)

        for dt in range(DT):
            sch.dma("y", gate_wT[:, dt, :], gw_sc[dt], waits=[p0a],
                    sem=sem("gwT"), transpose=True)
        gateT_ok = sem("gwT").tok()

        fblocks = []
        for hb in range(H):
            for p in range(4):
                fblocks.append((W_in[p][hb * 128:(hb + 1) * 128, :],
                                wT_sc[p][hb], ("w", p, hb), True))
        for e_ in range(E):
            for ob in range(22):
                fblocks.append((w1_in[e_][ob * 128:(ob + 1) * 128, :],
                                w1T_sc[e_, ob], ("w1", e_, ob), True))
            for it in range(I_MOE // 128):
                fblocks.append((
                    w2_in[e_][:, it * 128:(it + 1) * 128].rearrange(
                        "(hb pp) c -> pp hb c", pp=128),
                    w2T_sc[e_, it], ("w2", e_, it), False))
        for ob in range(44):
            fblocks.append((s1_in[ob * 128:(ob + 1) * 128, :],
                            s1T_sc[ob], ("s1", ob), True))
        for it in range(IS // 128):
            fblocks.append((
                s2_in[:, it * 128:(it + 1) * 128].rearrange(
                    "(hb pp) c -> pp hb c", pp=128),
                s2T_sc[it], ("s2", it), False))

        fact_tok = {}
        cast_tok = [None] * len(fblocks)
        fin_free = [None] * NFIN
        fstate = dict(ci=0, ti=0, fout_i=0, fout_free=[None, None])

        def fact_cast():
            while (fstate["ci"] < len(fblocks)
                   and fstate["ci"] < fstate["ti"] + NFIN):
                i = fstate["ci"]
                fstate["ci"] += 1
                src, dst, key, flat = fblocks[i]
                si = i % NFIN
                cdst = (fin[si][:, :] if flat else
                        fin[si].rearrange("p (hb c) -> p hb c", c=128))
                cast_tok[i] = sch.dma("g", cdst, src,
                                      waits=[fin_free[si]],
                                      sem=sem("fin%d" % si))

        def fact_emit(n):
            end = min(len(fblocks), fstate["ti"] + n)
            while fstate["ti"] < end:
                i = fstate["ti"]
                src, dst, key, flat = fblocks[i]
                si = i % NFIN
                so = fstate["fout_i"] % 2
                fstate["fout_i"] += 1
                cast = cast_tok[i]
                glast = None
                evs = []
                for j in range(4):
                    pj = j % 2
                    for k in range(4):
                        dt = 4 * j + k
                        w = []
                        if dt == 0:
                            w = [cast]
                            if i == 0:
                                w.append(p0a)
                        if k == 0:
                            w.append(fr["psA%d" % pj])
                        glast = sch.op("t", lambda e, dt=dt, pj=pj, k=k, si=si: e.matmul(
                            psA[:, pj * 512 + k * 128:pj * 512 + (k + 1) * 128],
                            fin[si][:, dt * 128:(dt + 1) * 128], idenb[:, :],
                            start=True, stop=True),
                            waits=w, inc=(tP if k == 3 else None))
                    eng = "v" if (fstate["fout_i"] + j) % 2 == 0 else "s"

                    def _ev(e, j=j, pj=pj, so=so):
                        dst_ = fout[so][:, 4 * j:4 * (j + 1), :]
                        src_ = psA[:, pj * 512:(pj + 1) * 512].rearrange(
                            "p (a c) -> p a c", c=128)
                        return (e.tensor_copy(dst_, src_)
                                if hasattr(e, "tensor_copy") else e.copy(dst_, src_))
                    ev = sch.op(eng, _ev,
                                waits=[glast, fstate["fout_free"][so]],
                                inc=(vP if eng == "v" else sP))
                    fr["psA%d" % pj] = ev
                    evs.append(ev)
                fstate["ti"] += 1
                fin_free[si] = glast
                st = sch.dma("s", dst, fout[so][:, :, :], waits=evs,
                             sem=sem("fout%d" % so))
                fstate["fout_free"][so] = st
                fact_tok[key] = st
                fact_cast()

        fact_cast()
        fact_emit(4)

        with ExitStack() as p0:
            e0 = p0.enter_context
            a_sb = e0(nc.sbuf_tensor("a_sb", [R, NL, DT, 128], F32))
            b_sb = e0(nc.sbuf_tensor("b_sb", [128, DT, NL, R], F32))
            st_i = e0(nc.sbuf_tensor("st_i", [1, NL], I32))
            st_f = e0(nc.sbuf_tensor("st_f", [1, NL], F32))
            stb = e0(nc.sbuf_tensor("stb", [16, NL], F32))
            ids = e0(nc.sbuf_tensor("ids", [16, 2], F32))
            oh = e0(nc.sbuf_tensor("oh", [16, NL], F32))
            ohrep = e0(nc.sbuf_tensor("ohrep", [128, B * NL], F32))
            aeff = e0(nc.sbuf_tensor("aeff", [R, DT, 128], F32))
            beffn = e0(nc.sbuf_tensor("beffn", [128, DT, R], F32))
            beffst = e0(nc.sbuf_tensor("beffst", [R, D], BF16))
            ps0 = e0(nc.psum_tensor("ps0", [128, 512], F32))

            tst = sch.dma("g", st_i[:, :], starts_in[:, :], sem=sem("st"))
            c1 = sch.op("v", lambda e: e.tensor_copy(st_f[:, :], st_i[:, :]),
                        waits=[tst], inc=vP)
            c2 = sch.op("t", lambda e: e.matmul(ps0[0:16, 0:NL], ones16[:, :],
                                                st_f[:, :], start=True, stop=True),
                        waits=[c1, p0a], inc=tP)
            c3 = sch.op("v", lambda e: e.tensor_copy(stb[:, :], ps0[0:16, 0:NL]),
                        waits=[c2], inc=vP)
            c4 = sch.op("v", lambda e: e.tensor_scalar(stb[:, :], stb[:, :],
                                                       iota16[:, 0:1], None,
                                                       op0=ALU.is_le),
                        waits=[c3], inc=vP)
            c5 = sch.op("v", lambda e: e.reduce_sum(ids[:, 0:1], stb[:, :], axis=AX.X),
                        waits=[c4], inc=vP)
            c6 = sch.op("v", lambda e: e.tensor_scalar(ids[:, 1:2], ids[:, 0:1],
                                                       -1.0, 0.0, op0=ALU.add,
                                                       op1=ALU.max),
                        waits=[c5], inc=vP)
            c7 = sch.op("v", lambda e: e.tensor_scalar(ids[:, 1:2], ids[:, 1:2],
                                                       3.0, None, op0=ALU.min),
                        waits=[c6], inc=vP)
            c8 = sch.op("v", lambda e: e.tensor_scalar(oh[:, :], iota4r[:, :],
                                                       ids[:, 1:2], None,
                                                       op0=ALU.is_equal),
                        waits=[c7], inc=vP)
            c9s = []
            for b in range(B):
                tt_ = sch.op("t", lambda e, b=b: e.matmul(
                    ps0[:, 256 + b * NL:256 + (b + 1) * NL],
                    erow[:, b * 128:(b + 1) * 128], oh[:, :],
                    start=True, stop=True), waits=[c8, c3], inc=tP)
                c9s.append(tt_)
            c10 = sch.op("v", lambda e: e.tensor_copy(ohrep[:, :],
                                                      ps0[:, 256:256 + B * NL]),
                         waits=c9s, inc=vP)

            ps_free = c10
            next_a = None
            next_b = None
            bst_free = None
            for p in range(4):
                ta = tb = None
                for j in range(NL):
                    ta = sch.dma("g", a_sb[:, j, :, :],
                                 A_in[p][j].rearrange("r (dt c) -> r dt c", c=128),
                                 waits=[next_a] if j == 0 else [], sem=sem("a"))
                    tb = sch.dma("g", b_sb[:, :, j, :],
                                 B_in[p][j].rearrange("(ot pp) r -> pp ot r", pp=128),
                                 waits=[next_b] if j == 0 else [], sem=sem("b"))
                for b in range(B):
                    pb = p * B + b
                    prev = None
                    for j in range(NL):
                        if j == 0:
                            prev = sch.op("v", lambda e, j=j, b=b:
                                          e.tensor_scalar(aeff[:, :, :], a_sb[:, j, :, :],
                                                          ohrep[0:16, b * NL + j:b * NL + j + 1],
                                                          None, op0=ALU.mult),
                                          waits=[c10, ta, next_a], inc=vP)
                        else:
                            prev = sch.op("v", lambda e, j=j, b=b:
                                          e.scalar_tensor_tensor(aeff[:, :, :], a_sb[:, j, :, :],
                                                                 ohrep[0:16, b * NL + j:b * NL + j + 1],
                                                                 aeff[:, :, :],
                                                                 op0=ALU.mult, op1=ALU.add),
                                          waits=[prev], inc=vP)
                    for c in range(DT):
                        tt_ = sch.op("t", lambda e, c=c: e.transpose(
                            ps0[:, 0:R], aeff[:, c, :], iden[0:16, 0:16]),
                            waits=[prev, ps_free], inc=tP)
                        ps_free = sch.op("v", lambda e, c=c, b=b, p=p: e.tensor_copy(
                            aeffT[:, b * DT + c, p * R:(p + 1) * R], ps0[:, 0:R]),
                            waits=[tt_], inc=vP)
                    next_a = ps_free
                    prev2 = None
                    for j in range(NL):
                        if j == 0:
                            prev2 = sch.op("v", lambda e, j=j, b=b:
                                           e.tensor_scalar(beffn[:, :, :], b_sb[:, :, j, :],
                                                           ohrep[:, b * NL + j:b * NL + j + 1],
                                                           None, op0=ALU.mult),
                                           waits=[c10, tb, next_b], inc=vP)
                        else:
                            prev2 = sch.op("v", lambda e, j=j, b=b:
                                           e.scalar_tensor_tensor(beffn[:, :, :], b_sb[:, :, j, :],
                                                                  ohrep[:, b * NL + j:b * NL + j + 1],
                                                                  beffn[:, :, :],
                                                                  op0=ALU.mult, op1=ALU.add),
                                           waits=[prev2], inc=vP)
                    for ot in range(DT):
                        tt_ = sch.op("t", lambda e, ot=ot: e.transpose(
                            ps0[0:R, 0:128], beffn[:, ot, :], iden[:, :]),
                            waits=[prev2, ps_free], inc=tP)
                        ps_free = sch.op("v", lambda e, ot=ot: e.tensor_copy(
                            beffst[:, ot * 128:(ot + 1) * 128], ps0[0:R, 0:128]),
                            waits=[tt_] + ([bst_free] if ot == 0 else []), inc=vP)
                    bst_free = sch.dma("g", beff_sc[pb], beffst[:, :],
                                       waits=[ps_free], sem=sem("bst"))
                    next_b = ps_free
            lora_ok_v = vP.tok()
            lora_ok_t = tP.tok()
            beff_ok = sem("bst").tok()

        p1s = ExitStack()
        xt = p1s.enter_context(nc.sbuf_tensor("xt", [128, PB, D], F32))
        xnt = p1s.enter_context(nc.sbuf_tensor("xnt", [128, D], BF16))
        lnst = p1s.enter_context(nc.sbuf_tensor("lnst", [128, PB, 6], F32))
        NT = TCORE // 128
        a4_tok = [None] * PB
        a9_tok = [None] * PB
        st_prev = None
        for i in range(NT):
            pb_ = i % PB
            rows = slice(i * 128, (i + 1) * 128)
            ld = sch.dma("s", xt[:, pb_, :], x_flat[rows, :],
                         waits=[a9_tok[pb_]] if i >= PB else
                         [lora_ok_v, lora_ok_t, beff_ok],
                         sem=sem("x1_%d" % pb_))
            a1 = sch.op("v", lambda e, pb_=pb_: e.reduce_sum(
                lnst[:, pb_, 0:1], xt[:, pb_, :], axis=AX.X), waits=[ld], inc=vP)
            a2 = sch.op("s", lambda e, pb_=pb_: e.activation(
                xnt[:, :], xt[:, pb_, :], AF.Square,
                accum_out=lnst[:, pb_, 1:2]),
                waits=[ld, a4_tok[pb_], st_prev], inc=sP)
            a3 = sch.op("v", lambda e, pb_=pb_: e.tensor_scalar(
                lnst[:, pb_, 2:3], lnst[:, pb_, 0:1], 1.0 / D, None, op0=ALU.mult),
                waits=[a1], inc=vP)
            a4 = sch.op("v", lambda e, pb_=pb_: e.tensor_scalar(
                lnst[:, pb_, 3:4], lnst[:, pb_, 1:2], 1.0 / D, float(EPS),
                op0=ALU.mult, op1=ALU.add),
                waits=[a2], inc=vP)
            a5 = sch.op("v", lambda e, pb_=pb_: e.tensor_mul(
                lnst[:, pb_, 5:6], lnst[:, pb_, 2:3], lnst[:, pb_, 2:3]),
                waits=[a3], inc=vP)
            a6 = sch.op("v", lambda e, pb_=pb_: e.tensor_sub(
                lnst[:, pb_, 3:4], lnst[:, pb_, 3:4], lnst[:, pb_, 5:6]),
                waits=[a4, a5], inc=vP)
            a7 = sch.op("s", lambda e, pb_=pb_: e.activation(
                lnst[:, pb_, 4:5], lnst[:, pb_, 3:4], AF.Ln),
                waits=[a6], inc=sP)
            a8 = sch.op("s", lambda e, pb_=pb_: e.activation(
                lnst[:, pb_, 4:5], lnst[:, pb_, 4:5], AF.Exp, bias=0.0, scale=-0.5),
                waits=[a7], inc=sP)
            a9 = sch.op("v", lambda e, pb_=pb_: e.tensor_scalar(
                xnt[:, :], xt[:, pb_, :], lnst[:, pb_, 2:3], lnst[:, pb_, 4:5],
                op0=ALU.subtract, op1=ALU.mult),
                waits=[a8], inc=vP)
            st = sch.dma("s", xn_sc[rows, :], xnt[:, :], waits=[a9],
                         sem=sem("xnst"))
            a4_tok[pb_] = a4
            a9_tok[pb_] = a9
            st_prev = st
            fact_emit(3)
        p1_done = sem("xnst").tok()
        p1s.close()

        with ExitStack() as p2:
            e2 = p2.enter_context
            xnT = e2(nc.sbuf_tensor("xnT", [128, DT, S], BF16, is_transpose_dest=True))
            beffTb = e2(nc.sbuf_tensor("beffTb", [R, 3, D], BF16))
            qs_sl = e2(nc.sbuf_tensor("qs_sl", [128, 2, DT, 128], BF16))
            ks_sl = e2(nc.sbuf_tensor("ks_sl", [128, 2, DT, 128], BF16))
            vs_sl = e2(nc.sbuf_tensor("vs_sl", [128, 2, DT, 128], BF16))
            ropeC = e2(nc.sbuf_tensor("ropeC", [128, S], BF16))
            ropeS = e2(nc.sbuf_tensor("ropeS", [128, S], BF16))
            masks = e2(nc.sbuf_tensor("masks", [128, 4, 512], BF16))
            qraw = e2(nc.sbuf_tensor("qraw", [128, 512], BF16))
            tmp1 = e2(nc.sbuf_tensor("tmp1", [128, 512], BF16))
            qhT = e2(nc.sbuf_tensor("qhT", [128, S], BF16))
            khT = e2(nc.sbuf_tensor("khT", [128, S], BF16))
            vhT = e2(nc.sbuf_tensor("vhT", [128, 512], F32))
            vh = e2(nc.sbuf_tensor("vh", [128, DT, 128], BF16))
            pT = e2(nc.sbuf_tensor("pT", [128, 4, 512], BF16))
            rbc = e2(nc.sbuf_tensor("rbc", [128, 512], F32))
            lowT = e2(nc.sbuf_tensor("lowT", [R, 3 * S], BF16))
            ahT = e2(nc.sbuf_tensor("ahT", [128, 2, 512], BF16))
            lg_sb = e2(nc.sbuf_tensor("lg_sb", [128, E], F32))
            lgst = e2(nc.sbuf_tensor("lgst", [128, 2], F32))
            probs = e2(nc.sbuf_tensor("probs", [128, E], F32))
            top8 = e2(nc.sbuf_tensor("top8", [128, 8], F32))
            combst = e2(nc.sbuf_tensor("combst", [E, 128], F32))
            aT = e2(nc.sbuf_tensor("aT", [128, 2, 2, 512], BF16))
            wost = e2(nc.sbuf_tensor("wost", [128, 512], F32))
            wo_blk = e2(nc.sbuf_tensor("wo_blk", [128, 2, DT, 128], BF16))
            psB = e2(nc.psum_tensor("psB", [128, 512], F32))
            psC = e2(nc.psum_tensor("psC", [128, 512], F32))
            psD = e2(nc.psum_tensor("psD", [128, 512], F32))
            psE = e2(nc.psum_tensor("psE", [128, 512], F32))

            def slot_ps(slot):
                return (psB[:, :], psC[:, :],
                        psA[:, 1024:1536], psA[:, 1536:2048])[slot]

            def slot_key(slot):
                return ("psB", "psC", "psA2", "psA3")[slot]

            cr = [sch.dma("s", ropeC[:, :], ropeC_in[:, :],
                          waits=[lora_ok_v, lora_ok_t, p1_done], sem=sem("rope")),
                  sch.dma("s", ropeS[:, :], ropeS_in[:, :], sem=sem("rope")),
                  sch.dma("s", masks[:, :, :], mask_in[:, :, :], sem=sem("rope"))]
            consts_ok = cr

            slab_free = [None, None]

            def load_slabs(h):
                hp = h % 2
                toks = []
                for p, (buf, name) in enumerate(((qs_sl, "qs"), (ks_sl, "ks"),
                                                 (vs_sl, "vs"))):
                    w = [fact_tok[("w", p, h)]]
                    if p == 0:
                        w += [slab_free[hp], lora_ok_v, lora_ok_t, p1_done]
                    toks.append(sch.dma("y", buf[:, hp, :, :], wT_sc[p][h],
                                        waits=w, sem=sem("%s%d" % (name, hp))))
                return toks

            for b in range(B):
                brow = b * S
                xnT_toks = []
                for dt in range(DT):
                    xnT_toks.append(sch.dma(
                        "y", xnT[:, dt, :],
                        xn_sc[brow:brow + S, dt * 128:(dt + 1) * 128],
                        waits=[p1_done, fr["xnT"], lora_ok_t]
                        if dt == 0 else [], sem=sem("xnT"), transpose=True))
                xnT_ok = xnT_toks[-1]
                bt = sch.dma("g", beffTb[:, :, :],
                             beff_sc.rearrange("(p bb) r d -> bb r p d", bb=B)[b][:, 0:3, :],
                             waits=[beff_ok, fr["beffTb"], p1_done], sem=sem("bt"))
                slab_toks = load_slabs(0)

                for tt in range(S // 128):
                    g_t = brow + tt * 128
                    mm = None
                    for dt in range(DT):
                        mm = sch.op("t", lambda e, dt=dt, tt=tt: e.matmul(
                            psC[:, 0:E], xnT[:, dt, tt * 128:(tt + 1) * 128],
                            gate_wT[:, dt, :], start=(dt == 0), stop=(dt == DT - 1)),
                            waits=[xnT_ok, gateT_ok, fr["psC"]] if dt == 0 else [],
                            inc=(tP if dt == DT - 1 else None))
                    r1 = sch.op("v", lambda e: e.tensor_copy(lg_sb[:, :], psC[:, 0:E]),
                                waits=[mm], inc=vP)
                    r2 = sch.op("v", lambda e: e.reduce_max(lgst[:, 0:1], lg_sb[:, :],
                                                            axis=AX.X), waits=[r1], inc=vP)
                    r3 = sch.op("v", lambda e: e.tensor_scalar(
                        lgst[:, 0:1], lgst[:, 0:1], -1.0, None, op0=ALU.mult),
                        waits=[r2], inc=vP)
                    r4 = sch.op("s", lambda e: e.activation(
                        probs[:, :], lg_sb[:, :], AF.Exp, bias=lgst[:, 0:1], scale=1.0),
                        waits=[r3], inc=sP)
                    r5 = sch.op("v", lambda e: e.reduce_sum(lgst[:, 1:2], probs[:, :],
                                                            axis=AX.X), waits=[r4], inc=vP)
                    r6 = sch.op("v", lambda e: e.reciprocal(lgst[:, 1:2], lgst[:, 1:2]),
                                waits=[r5], inc=vP)
                    r7 = sch.op("v", lambda e: e.tensor_scalar(
                        probs[:, :], probs[:, :], lgst[:, 1:2], None, op0=ALU.mult),
                        waits=[r6], inc=vP)
                    r8 = sch.op("v", lambda e: e.max(out=top8[:, :], in_=probs[:, :]),
                                waits=[r7], inc=vP)
                    r9 = sch.op("v", lambda e: e.tensor_scalar(
                        lg_sb[:, :], probs[:, :], top8[:, 5:6], None, op0=ALU.is_ge),
                        waits=[r8, r1], inc=vP)
                    r10 = sch.op("v", lambda e: e.tensor_mul(
                        lg_sb[:, :], lg_sb[:, :], probs[:, :]), waits=[r9], inc=vP)
                    r11 = sch.op("t", lambda e: e.transpose(
                        psC[0:E, 0:128], lg_sb[:, :], iden[:, :]),
                        waits=[r10], inc=tP)
                    r12 = sch.op("v", lambda e: e.tensor_copy(
                        combst[:, :], psC[0:E, 0:128]),
                        waits=[r11, fr["combst"]], inc=vP)
                    fr["psC"] = r12
                    fr["combst"] = sch.dma("s", comb_sc[:, g_t:g_t + 128],
                                           combst[:, :], waits=[r12], sem=sem("comb"))
                    fact_emit(1)

                for p in range(3):
                    for t4 in range(S // 512):
                        mm = None
                        for dt in range(DT):
                            mm = sch.op("t", lambda e, p=p, dt=dt, t4=t4, b=b: e.matmul(
                                psB[0:R, :], aeffT[:, b * DT + dt, p * R:(p + 1) * R],
                                xnT[:, dt, t4 * 512:(t4 + 1) * 512],
                                start=(dt == 0), stop=(dt == DT - 1)),
                                waits=[xnT_ok, fr["psB"], lora_ok_v] if dt == 0 else [],
                                inc=(tP if dt == DT - 1 else None))
                        ev = sch.op("v", lambda e, p=p, t4=t4: e.tensor_copy(
                            lowT[:, p * S + t4 * 512:p * S + (t4 + 1) * 512],
                            psB[0:R, :]), waits=[mm], inc=vP)
                        fr["psB"] = ev
                        fact_emit(1)
                low_ok = vP.tok()

                att_store = []
                for h in range(H):
                    hp = h % 2
                    nxt_toks = load_slabs(h + 1) if h + 1 < H else None
                    sq, sk, sv = slab_toks
                    if nxt_toks is not None:
                        slab_toks = nxt_toks

                    def proj(slab, slab_tok, pidx, out_bf, roped, h=h):
                        mm_last = []
                        for t4 in range(S // 512):
                            qsl = slice(t4 * 512, (t4 + 1) * 512)
                            for dt in range(DT):
                                w = []
                                if dt == 0:
                                    w = [slab_tok, xnT_ok, fr["psA%d" % t4]]
                                sch.op("t", lambda e, dt=dt, qsl=qsl, slab=slab: e.matmul(
                                    psA[:, qsl], slab[:, dt, :], xnT[:, dt, qsl],
                                    start=(dt == 0), stop=False), waits=w)
                            mm_last.append(sch.op(
                                "t", lambda e, qsl=qsl, pidx=pidx, t4=t4, h=h: e.matmul(
                                    psA[:, qsl],
                                    beffTb[:, pidx, h * 128:(h + 1) * 128],
                                    lowT[:, pidx * S + t4 * 512:pidx * S + (t4 + 1) * 512],
                                    start=False, stop=True),
                                waits=[low_ok, bt] if t4 == 0 else [],
                                inc=tP))
                        last = None
                        for t4 in range(S // 512):
                            qsl = slice(t4 * 512, (t4 + 1) * 512)
                            if not roped:
                                ev = sch.op("v", lambda e, qsl=qsl: e.tensor_copy(
                                    vhT[:, :], psA[:, qsl]),
                                    waits=[mm_last[t4], fr["vhT"]], inc=vP)
                                tt_ = None
                                for c4 in range(4):
                                    w = [ev]
                                    if c4 == 0:
                                        w.append(fr["psE"])
                                    tt_ = sch.op("t", lambda e, c4=c4: e.transpose(
                                        psE[:, c4 * 128:(c4 + 1) * 128],
                                        vhT[:, c4 * 128:(c4 + 1) * 128],
                                        iden[:, :]),
                                        waits=w, inc=(tP if c4 == 3 else None))
                                vtr = sch.op("v", lambda e, t4=t4: e.tensor_copy(
                                    vh[:, t4 * 4:(t4 + 1) * 4, :],
                                    psE[:, 0:512].rearrange("p (a c) -> p a c", c=128)),
                                    waits=[tt_], inc=vP)
                                fr["psE"] = vtr
                                fr["vhT"] = tt_
                                fr["psA%d" % t4] = ev
                                last = vtr
                                continue
                            ev = sch.op("v", lambda e, qsl=qsl: e.tensor_copy(
                                qraw[:, :], psA[:, qsl]),
                                waits=[mm_last[t4], fr["qraw"]], inc=vP)
                            pm = sch.op("t", lambda e: e.matmul(
                                psB[:, :], permb[:, :], qraw[:, :],
                                start=True, stop=True),
                                waits=[ev, fr["psB"]] + consts_ok, inc=tP)
                            w1_ = sch.op("v", lambda e, qsl=qsl: e.tensor_mul(
                                tmp1[:, :], ropeS[:, qsl], psB[:, :]),
                                waits=[pm], inc=vP)
                            fr["psB"] = w1_
                            w2_ = sch.op("v", lambda e, qsl=qsl: e.tensor_mul(
                                qraw[:, :], ropeC[:, qsl], qraw[:, :]),
                                waits=[w1_], inc=vP)
                            last = sch.op(
                                "v", lambda e, qsl=qsl, o=out_bf: e.tensor_add(
                                    o[:, qsl], qraw[:, :], tmp1[:, :]),
                                waits=[w2_], inc=vP)
                            fr["qraw"] = last
                            fr["psA%d" % t4] = ev
                        return last

                    q_ok = proj(qs_sl[:, hp], sq, 0, qhT, True)
                    k_ok = proj(ks_sl[:, hp], sk, 1, khT, True)
                    v_ready = proj(vs_sl[:, hp], sv, 2, None, False)
                    slab_free[hp] = tP.tok()

                    at = None
                    for qt in range(4):
                        qsl = slice(qt * 512, (qt + 1) * 512)
                        KT = 4 * qt + 4
                        LA = 3
                        pt_ready = {}

                        def emit_sc(kt, qt=qt, qsl=qsl):
                            slot = kt % 4
                            ps = slot_ps(slot)
                            sc = sch.op("t", lambda e, kt=kt, qsl=qsl, ps=ps: e.matmul(
                                ps, khT[:, kt * 128:(kt + 1) * 128],
                                qhT[:, qsl], start=True, stop=True),
                                waits=[q_ok, k_ok, fr[slot_key(slot)]], inc=tP)
                            ex = sch.op("s", lambda e, slot=slot, ps=ps: e.activation(
                                pT[:, slot, :], ps, AF.Exp,
                                bias=neg8[:, 0:1], scale=float(SCALE)),
                                waits=[sc, fr["pT%d" % slot], neg8_ok], inc=sP)
                            fr[slot_key(slot)] = ex
                            pr = ex
                            if kt >= 4 * qt:
                                pr = sch.op("v", lambda e, slot=slot, mi=kt - 4 * qt: e.tensor_mul(
                                    pT[:, slot, :], pT[:, slot, :], masks[:, mi, :]),
                                    waits=[ex] + consts_ok, inc=vP)
                            pt_ready[kt] = pr

                        for kt in range(min(LA, KT)):
                            emit_sc(kt)
                        av_last = None
                        dn_last = None
                        for kt in range(KT):
                            if kt + LA < KT:
                                emit_sc(kt + LA)
                            slot = kt % 4
                            av_last = sch.op("t", lambda e, kt=kt, slot=slot, KT=KT: e.matmul(
                                psD[:, :], vh[:, kt, :], pT[:, slot, :],
                                start=(kt == 0), stop=(kt == KT - 1)),
                                waits=[pt_ready[kt]] +
                                      ([v_ready, fr["psD"]] if kt == 0 else []),
                                inc=(tP if kt == KT - 1 else None))
                            dn_last = sch.op("t", lambda e, kt=kt, slot=slot, KT=KT: e.matmul(
                                psE[:, :], onesb[:, :], pT[:, slot, :],
                                start=(kt == 0), stop=(kt == KT - 1)),
                                waits=([fr["psE"]] if kt == 0 else []),
                                inc=tP)
                            fr["pT%d" % slot] = dn_last
                        rc = sch.op("v", lambda e: e.reciprocal(rbc[:, :], psE[:, :]),
                                    waits=[dn_last, fr["rbc"]], inc=vP)
                        fr["psE"] = rc
                        aslot = qt % 2
                        at = sch.op("v", lambda e, aslot=aslot: e.tensor_mul(
                            ahT[:, aslot, :], psD[:, :], rbc[:, :]),
                            waits=[rc, av_last, fr["ahT%d" % aslot]], inc=vP)
                        fr["psD"] = at
                        fr["rbc"] = at
                        ast = sch.dma("s", attnT_sc[b, h][:, qsl],
                                      ahT[:, aslot, :], waits=[at],
                                      sem=sem("ast%d" % aslot))
                        fr["ahT%d" % aslot] = ast
                        fact_emit(3)
                    att_store.append(ast)
                    fact_emit(2)
                attn_ok = [fr["ahT0"], fr["ahT1"]]
                fr["xnT"] = tP.tok()
                fr["beffTb"] = tP.tok()

                for t4 in range(S // 512):
                    t4sl = slice(t4 * 512, (t4 + 1) * 512)
                    mm = None
                    for dq in range(8):
                        apar = dq % 2
                        lld = sch.dma(
                            "s", aT[:, apar, :, :],
                            attnT_sc[b][2 * dq:2 * dq + 2, :, t4sl].rearrange(
                                "h p s -> p h s"),
                            waits=attn_ok + [fr["aT%d" % apar]],
                            sem=sem("aT%d" % apar))
                        for k in range(2):
                            dt = 2 * dq + k
                            mm = sch.op("t", lambda e, dt=dt, k=k, b=b, apar=apar: e.matmul(
                                psB[0:R, :], aeffT[:, b * DT + dt, 3 * R:4 * R],
                                aT[:, apar, k, :],
                                start=(dt == 0), stop=(dt == DT - 1)),
                                waits=[lld] + ([fr["psB"]] if dt == 0 else []),
                                inc=(tP if k == 1 else None))
                        fr["aT%d" % apar] = mm
                    ev = sch.op("v", lambda e, t4=t4: e.tensor_copy(
                        lowT[:, t4 * 512:(t4 + 1) * 512],
                        psB[0:R, :]), waits=[mm], inc=vP)
                    fr["psB"] = ev
                    fact_emit(1)
                lowo_ok = vP.tok()
                bt_o = sch.dma("g", beffTb[:, 0, :], beff_sc[3 * B + b],
                               waits=[beff_ok, fr["beffTb"]], sem=sem("bt"))
                fr["beffTb"] = None

                for og in range(8):
                    wg = None
                    for i in range(2):
                        wg = sch.dma("y", wo_blk[:, i, :, :], wT_sc[3][og * 2 + i],
                                     waits=[fact_tok[("w", 3, og * 2 + i)],
                                            fr["wo_blk"] if i == 0 else None],
                                     sem=sem("wo"))
                    wo_ok = wg
                    for t4 in range(S // 512):
                        t4sl = slice(t4 * 512, (t4 + 1) * 512)
                        for dq in range(8):
                            apar = dq % 2
                            lld = sch.dma(
                                "s", aT[:, apar, :, :],
                                attnT_sc[b][2 * dq:2 * dq + 2, :, t4sl].rearrange(
                                    "h p s -> p h s"),
                                waits=[fr["aT%d" % apar]], sem=sem("aT%d" % apar))
                            mmw = None
                            for k in range(2):
                                dt = 2 * dq + k
                                for i in range(2):
                                    w = []
                                    if k == 0 and i == 0:
                                        w.append(lld)
                                    if dt == 0:
                                        w.append(wo_ok)
                                        w.append(fr["psA%d" % i])
                                    mmw = sch.op("t", lambda e, dt=dt, i=i, apar=apar, k=k: e.matmul(
                                        psA[:, i * 512:(i + 1) * 512],
                                        wo_blk[:, i, dt, :], aT[:, apar, k, :],
                                        start=(dt == 0), stop=False),
                                        waits=w,
                                        inc=(tP if i == 1 else None))
                            fr["aT%d" % apar] = mmw
                        lora_last = None
                        for i in range(2):
                            lora_last = sch.op("t", lambda e, i=i, og=og, t4=t4: e.matmul(
                                psA[:, i * 512:(i + 1) * 512],
                                beffTb[:, 0, (og * 2 + i) * 128:(og * 2 + i + 1) * 128],
                                lowT[:, t4 * 512:(t4 + 1) * 512],
                                start=False, stop=True),
                                waits=[lowo_ok, bt_o],
                                inc=(tP if i == 1 else None))
                        ev = None
                        for i in range(2):
                            ev = sch.op("v", lambda e, i=i: e.tensor_copy(
                                wost[:, :], psA[:, i * 512:(i + 1) * 512]),
                                waits=[lora_last, fr["wost"]], inc=vP)
                            st = sch.dma("s",
                                         attn_sc[og * 2 + i][:, brow + t4 * 512:
                                                             brow + (t4 + 1) * 512],
                                         wost[:, :], waits=[ev],
                                         sem=sem("wst"))
                            fr["wost"] = st
                            fr["psA%d" % i] = ev
                        fact_emit(1)
                    fr["wo_blk"] = tP.tok()
                fr["beffTb"] = tP.tok()
            fact_emit(10 ** 9)
            attn_all = [sem("ast0").tok(), sem("ast1").tok()]
            wst_all = [sem("wst").tok()]
            comb_all_dA = sem("comb").tok()
            p2_t = tP.tok()
            p2_v = vP.tok()
            p2_s = sP.tok()
            p2_dA = wst_all + attn_all + [comb_all_dA,
                                          sem("fout0").tok(), sem("fout1").tok()]

        with ExitStack() as p3:
            e3 = p3.enter_context
            xnB = e3(nc.sbuf_tensor("xnB", [128, DT, TBLK], BF16, is_transpose_dest=True))
            acc = e3(nc.sbuf_tensor("acc", [128, TBLK // 128, D], F32))
            act = e3(nc.sbuf_tensor("act", [128, 11, TBLK], BF16))
            w1b = e3(nc.sbuf_tensor("w1b", [128, 2, 2, DT, 128], BF16))
            w2s = e3(nc.sbuf_tensor("w2s", [128, 2, 11, 512], BF16))
            cbs = e3(nc.sbuf_tensor("cbs", [128, TBLK], F32))
            sg = e3(nc.sbuf_tensor("sg", [128, 2, 512], BF16))
            ucb = e3(nc.sbuf_tensor("ucb", [128, 2, 512], BF16))
            mrg = e3(nc.sbuf_tensor("mrg", [128, 256], F32))
            eall = e3(nc.sbuf_tensor("eall", [E, 2, 128], F32))
            combB = e3(nc.sbuf_tensor("combB", [E, 2, 512], F32))
            att3 = e3(nc.sbuf_tensor("att3", [128, 2, 4, 128], F32))
            psB3 = e3(nc.psum_tensor("psB3", [128, 512], F32))
            psC3 = e3(nc.psum_tensor("psC3", [128, 512], F32))
            psD3 = e3(nc.psum_tensor("psD3", [128, 512], F32))
            psE3 = e3(nc.psum_tensor("psE3", [128, 512], F32))
            GU = [((psB3, "B"), (psC3, "C")), ((psD3, "D"), (psE3, "E"))]

            bar = [p2_t, p2_v, p2_s] + p2_dA

            fr3 = {k: None for k in ("B", "C", "D", "E", "dn0", "dn1",
                                     "xnB", "w2s", "sg0", "sg1", "uc0", "uc1",
                                     "cbs", "mrg", "slab0", "slab1", "act",
                                     "combB", "w2s0", "w2s1", "att30", "att31")}
            out_store = [[None] * (TBLK // 128) for _ in range(NBLK)]
            experts = [("moe", i) for i in range(E)] + [("sh", 0), ("sh", 1)]

            for blk in range(NBLK):
                t0 = blk * TBLK
                xnB_toks = []
                for dt in range(DT):
                    xnB_toks.append(sch.dma(
                        "y", xnB[:, dt, :],
                        xn_sc[t0:t0 + TBLK, dt * 128:(dt + 1) * 128],
                        waits=([p1_done, fr3["xnB"]] + bar) if dt == 0 else [],
                        sem=sem("xnB"), transpose=True))
                xnB_ok = xnB_toks[-1]
                combB_tok = sch.dma("g", combB[:, :, :],
                                    comb_sc[:, t0:t0 + TBLK].rearrange(
                                        "r (a c) -> r a c", c=512),
                                    waits=[comb_all_dA, fr3["combB"]] + bar,
                                    sem=sem("cB"))
                acc_last = [None] * (TBLK // 128)

                for ei, (kind, eidx) in enumerate(experts):
                    if kind == "moe":
                        ep = eidx % 2
                        eall_tok = sch.dma(
                            "g", eall[:, ep, :],
                            eall_in[:, eidx * 128:(eidx + 1) * 128],
                            waits=bar + [fr3.get("eall%d" % ep)],
                            sem=sem("ea%d" % ep))
                        cb_mm0 = sch.op("t", lambda e, eidx=eidx, ep=ep: e.matmul(
                            psB3[:, :], eall[:, ep, :],
                            combB[:, 0, :], start=True, stop=True),
                            waits=[combB_tok, eall_tok, fr3["B"]], inc=tP)
                        cb_mm1 = sch.op("t", lambda e, eidx=eidx, ep=ep: e.matmul(
                            psC3[:, :], eall[:, ep, :],
                            combB[:, 1, :], start=True, stop=True),
                            waits=[fr3["C"]], inc=tP)
                        fr3["eall%d" % ep] = cb_mm1
                        cb0 = sch.op("v", lambda e: e.tensor_copy(
                            cbs[:, 0:512], psB3[:, :]), waits=[cb_mm0, fr3["cbs"]],
                            inc=vP)
                        cb1 = sch.op("v", lambda e: e.tensor_copy(
                            cbs[:, 512:1024], psC3[:, :]), waits=[cb_mm1], inc=vP)
                        fr3["B"] = cb0
                        fr3["C"] = cb1
                        cb_ok = [cb0, cb1]
                    else:
                        cb_ok = []

                    act_toks = []
                    for icg in range(11):
                        par = icg % 2
                        if kind == "moe":
                            gblk = w1T_sc[eidx, icg]
                            ublk = w1T_sc[eidx, 11 + icg]
                            gtk = fact_tok[("w1", eidx, icg)]
                            utk = fact_tok[("w1", eidx, 11 + icg)]
                        else:
                            gblk = s1T_sc[eidx * 11 + icg]
                            ublk = s1T_sc[22 + eidx * 11 + icg]
                            gtk = fact_tok[("s1", eidx * 11 + icg)]
                            utk = fact_tok[("s1", 22 + eidx * 11 + icg)]
                        gl = sch.dma("y", w1b[:, par, 0, :, :], gblk,
                                     waits=[gtk, fr3["slab%d" % par]] + bar,
                                     sem=sem("w1b%d" % par))
                        ul = sch.dma("y", w1b[:, par, 1, :, :], ublk,
                                     waits=[utk], sem=sem("w1b%d" % par))
                        for tsub in range(2):
                            (ps_g, kg), (ps_u, ku) = GU[tsub]
                            tsl = slice(tsub * 512, (tsub + 1) * 512)
                            g_last = None
                            for dt in range(DT):
                                w = [ul, xnB_ok, fr3[kg]] if dt == 0 else []
                                g_last = sch.op("t", lambda e, dt=dt, par=par, tsl=tsl, ps=ps_g: e.matmul(
                                    ps[:, :], w1b[:, par, 0, dt, :],
                                    xnB[:, dt, tsl], start=(dt == 0), stop=(dt == DT - 1)),
                                    waits=w, inc=(tP if dt == DT - 1 else None))
                            u_last = None
                            for dt in range(DT):
                                w = [ul, fr3[ku]] if dt == 0 else []
                                u_last = sch.op("t", lambda e, dt=dt, par=par, tsl=tsl, ps=ps_u: e.matmul(
                                    ps[:, :], w1b[:, par, 1, dt, :],
                                    xnB[:, dt, tsl], start=(dt == 0), stop=(dt == DT - 1)),
                                    waits=w, inc=(tP if dt == DT - 1 else None))
                            sgt = sch.op("s", lambda e, tsub=tsub, ps=ps_g: e.activation(
                                sg[:, tsub, :], ps[:, :], AF.Silu),
                                waits=[g_last, fr3["sg%d" % tsub]], inc=sP)
                            fr3[kg] = sgt
                            if kind == "moe":
                                uct = sch.op("v", lambda e, tsub=tsub, tsl=tsl, ps=ps_u: e.tensor_mul(
                                    ucb[:, tsub, :], ps[:, :], cbs[:, tsl]),
                                    waits=[u_last, fr3["uc%d" % tsub]] + cb_ok, inc=vP)
                            else:
                                uct = sch.op("v", lambda e, tsub=tsub, ps=ps_u: e.tensor_copy(
                                    ucb[:, tsub, :], ps[:, :]),
                                    waits=[u_last, fr3["uc%d" % tsub]], inc=vP)
                            fr3[ku] = uct
                            at3 = sch.op("v", lambda e, tsub=tsub, icg=icg, tsl=tsl: e.tensor_mul(
                                act[:, icg, tsl], sg[:, tsub, :], ucb[:, tsub, :]),
                                waits=[sgt, uct, fr3["act"]], inc=vP)
                            fr3["sg%d" % tsub] = at3
                            fr3["uc%d" % tsub] = at3
                            act_toks.append(at3)
                        fr3["slab%d" % par] = tP.tok()
                    act_ok = act_toks[-1]
                    if kind == "moe":
                        fr3["cbs"] = act_toks[-1]

                    for hq in range(4):
                        wp2 = (ei * 4 + hq) % 2
                        w2_toks = []
                        for it in range(11):
                            if kind == "moe":
                                src = w2T_sc[eidx, it][:, hq * 4:(hq + 1) * 4, :]
                                wtok = fact_tok[("w2", eidx, it)]
                            else:
                                src = s2T_sc[eidx * 11 + it][:, hq * 4:(hq + 1) * 4, :]
                                wtok = fact_tok[("s2", eidx * 11 + it)]
                            w = [wtok] + (([fr3["w2s%d" % wp2]] + bar)
                                          if it == 0 else [])
                            w2_toks.append(sch.dma(
                                "y", w2s[:, wp2, it, :].rearrange("p (a c) -> p a c", c=128),
                                src, waits=w, sem=sem("w2s%d" % wp2)))
                        w2_ok = w2_toks[-1]
                        for tt in range(TBLK // 128):
                            hp = tt % 2
                            ps_dn = psA[:, hp * 512:(hp + 1) * 512]
                            dn_last = None
                            fold_attn = (ei == 0)
                            for ic in range(11):
                                w = []
                                if ic == 0:
                                    w = [w2_ok, act_ok, fr3["dn%d" % hp]]
                                dn_last = sch.op("t", lambda e, ic=ic, tt=tt, wp2=wp2,
                                                 ps=ps_dn, fa=fold_attn: e.matmul(
                                    ps[:, :],
                                    act[:, ic, tt * 128:(tt + 1) * 128],
                                    w2s[:, wp2, ic, :],
                                    start=(ic == 0),
                                    stop=(ic == 10 and not fa)),
                                    waits=w,
                                    inc=(tP if (ic == 10 and not fold_attn) else None))
                            if fold_attn:
                                glob_t = t0 + tt * 128
                                trs = slice(glob_t, glob_t + 128)
                                a3p = tt % 2
                                a3ld = []
                                for c in range(4):
                                    a3ld.append(sch.dma(
                                        "g", att3[:, a3p, c, :],
                                        attn_sc[hq * 4 + c][:, trs],
                                        waits=(bar + [fr3["att3%d" % a3p]])
                                        if c == 0 else [], sem=sem("a3%d" % a3p)))
                                for c in range(4):
                                    dn_last = sch.op("t", lambda e, c=c, a3p=a3p, ps=ps_dn: e.matmul(
                                        ps[:, c * 128:(c + 1) * 128],
                                        att3[:, a3p, c, :], iden[:, :],
                                        start=False, stop=(c == 3)),
                                        waits=[a3ld[3]],
                                        inc=(tP if c == 3 else None))
                                fr3["att3%d" % a3p] = dn_last
                            hsl = slice(hq * 512, (hq + 1) * 512)
                            if ei == 0:
                                ev = sch.op("v", lambda e, tt=tt, hsl=hsl, ps=ps_dn: e.tensor_copy(
                                    acc[:, tt, hsl], ps[:, :]),
                                    waits=[dn_last,
                                           out_store[blk - 1][-1] if blk else None],
                                    inc=vP)
                            else:
                                ev = sch.op("v", lambda e, tt=tt, hsl=hsl, ps=ps_dn: e.tensor_add(
                                    acc[:, tt, hsl], acc[:, tt, hsl], ps[:, :]),
                                    waits=[dn_last, acc_last[tt]], inc=vP)
                            fr3["dn%d" % hp] = ev
                            acc_last[tt] = ev
                        fr3["w2s%d" % wp2] = tP.tok()
                    fr3["act"] = tP.tok()
                fr3["xnB"] = tP.tok()
                fr3["combB"] = tP.tok()

                for tt in range(TBLK // 128):
                    rows = slice(t0 + tt * 128, t0 + (tt + 1) * 128)
                    mtok = None
                    for half in range(8):
                        csl = slice(half * 256, (half + 1) * 256)
                        lx = sch.dma("s", mrg[:, :], x_flat[rows, csl],
                                     waits=[fr3["mrg"], acc_last[tt]], sem=sem("mrg"))
                        mtok = sch.op("v", lambda e, tt=tt, csl=csl: e.tensor_add(
                            acc[:, tt, csl], acc[:, tt, csl], mrg[:, :]),
                            waits=[lx], inc=vP)
                        fr3["mrg"] = mtok
                    st = sch.dma("s", out_flat[rows, :], acc[:, tt, :],
                                 waits=[mtok], sem=sem("out"))
                    out_store[blk][tt] = st
                    acc_last[tt] = None

        finals = [sem("out").tok()]
        sch.op("g", lambda e: e.engine_nop(), waits=finals)

        with nc.Block() as block:
            sch.emit(block)

    return nc, {}


def _host_constants():
    import ml_dtypes
    iden = np.eye(128, dtype=np.float32)
    idenb = np.eye(128, dtype=np.float32).astype(ml_dtypes.bfloat16)
    perm = np.zeros((128, 128), dtype=np.float32)
    for i in range(64):
        perm[2 * i, 2 * i + 1] = 1.0
        perm[2 * i + 1, 2 * i] = 1.0
    pos = np.arange(S, dtype=np.float64)
    invf = THETA ** (-np.arange(0, DH, 2, dtype=np.float64) / DH)
    ang = invf[:, None] * pos[None, :]
    C = np.empty((128, S), dtype=np.float32)
    Sg = np.empty((128, S), dtype=np.float32)
    C[0::2, :] = np.cos(ang)
    C[1::2, :] = np.cos(ang)
    Sg[0::2, :] = -np.sin(ang)
    Sg[1::2, :] = np.sin(ang)
    masks = np.zeros((128, 4, 512), dtype=np.float32)
    for r_ in range(4):
        for kp in range(128):
            q0 = 128 * r_ + kp
            if q0 < 512:
                masks[kp, r_, q0:] = 1.0
    eall = np.zeros((E, E * 128), dtype=np.float32)
    for e_ in range(E):
        eall[e_, e_ * 128:(e_ + 1) * 128] = 1.0
    return dict(
        c_iden=iden, c_idenb=np.asarray(idenb),
        c_perm=np.asarray(perm.astype(ml_dtypes.bfloat16)),
        c_onesb=np.asarray(np.ones((128, 128), dtype=np.float32).astype(ml_dtypes.bfloat16)),
        c_ropec=np.asarray(C.astype(ml_dtypes.bfloat16)),
        c_ropes=np.asarray(Sg.astype(ml_dtypes.bfloat16)),
        c_mask=np.asarray(masks.astype(ml_dtypes.bfloat16)),
        c_eall=eall,
        c_ones16=np.ones((1, 16), dtype=np.float32),
        c_iota16=np.arange(16, dtype=np.float32).reshape(16, 1),
        c_iota4r=np.broadcast_to(np.arange(NL, dtype=np.float32), (16, NL)).copy(),
    )


def _make_in_maps(inputs, consts):
    x = np.ascontiguousarray(np.asarray(inputs["L0_attn_x"], dtype=np.float32))
    starts = np.ascontiguousarray(
        np.asarray(inputs["L0_attn_lora_start_indices"], dtype=np.int32).reshape(1, NL))
    weights = {}
    for k, v in inputs.items():
        if k in ("L0_attn_x", "L0_attn_lora_start_indices"):
            continue
        weights[k] = np.ascontiguousarray(np.asarray(v, dtype=np.float32))
    in_maps = []
    for c in range(NCORES):
        m = dict(weights)
        m["L0_attn_x"] = np.ascontiguousarray(x[c * B:(c + 1) * B])
        m["L0_attn_lora_start_indices"] = starts
        er = np.zeros((16, B * 128), dtype=np.float32)
        for i in range(B):
            er[c * B + i, i * 128:(i + 1) * 128] = 1.0
        m["c_erow"] = er
        m.update(consts)
        in_maps.append(m)
    return in_maps


def kernel(**inputs):
    nc, _ = build()
    consts = _host_constants()
    in_maps = _make_in_maps(inputs, consts)
    res = run_bass_kernel_spmd(nc, in_maps, core_ids=list(range(NCORES)))
    return np.concatenate([r["out"] for r in res.results], axis=0)
